# version 30
# baseline (speedup 1.0000x reference)
import numpy as np
import ml_dtypes
from contextlib import ExitStack

import concourse.bass as bass
import concourse.mybir as mybir
from concourse.bass_utils import run_bass_kernel_spmd

F32 = mybir.dt.float32
BF16 = mybir.dt.bfloat16
AF = mybir.ActivationFunctionType
ALU = mybir.AluOpType
AX = mybir.AxisListType

D = 1024
FF = 4096
E = 4
RK = 4
DH = 64
NPAIR = 8
NCORES = 8
DCH = D // 128

ENGINES = ("sync", "tensor", "vector", "scalar", "gpsimd")
ATTN_STAGE = 0
LOOP_PART = 4


class Sem:
    def __init__(self, name, step):
        self.name, self.step, self.n, self.h = name, step, 0, None


class Buf:
    __slots__ = ("name", "ready", "readers")

    def __init__(self, name):
        self.name, self.ready, self.readers = name, None, []


NDMASEM = 8


class Prog:
    def __init__(self):
        self.ops = {e: [] for e in ENGINES}
        self.sems = {e: Sem("c_" + e, 1) for e in ENGINES}
        self.dma_pools = {q: [Sem(f"d_{q}{i}", 16) for i in range(NDMASEM)]
                          for q in ("sync", "scalar", "gpsimd")}
        self.dma_idx = {q: 0 for q in self.dma_pools}
        self.waited = {e: {} for e in ENGINES}

    def all_sems(self):
        out = list(self.sems.values())
        for pool in self.dma_pools.values():
            out.extend(pool)
        return out

    def _wait(self, eng, tok):
        if tok is None:
            return
        if isinstance(tok, list):
            for t in tok:
                self._wait(eng, t)
            return
        sem, val = tok
        if self.waited[eng].get(sem.name, 0) >= val:
            return
        self.waited[eng][sem.name] = val
        self.ops[eng].append(lambda ee, s=sem, v=val: ee.wait_ge(s.h, v))

    def emit(self, eng, fns, reads=(), writes=(), dma=False):
        for b in reads:
            self._wait(eng, b.ready)
        for b in writes:
            self._wait(eng, b.ready)
            for t in b.readers:
                self._wait(eng, t)
        if dma:
            toks = []
            for fn in fns:
                pool = self.dma_pools[eng]
                sem = pool[self.dma_idx[eng]]
                self.dma_idx[eng] = (self.dma_idx[eng] + 1) % NDMASEM
                if sem.n > 0:
                    self._wait(eng, (sem, sem.n))
                sem.n += 16
                self.ops[eng].append(
                    lambda ee, f=fn, s=sem: f(ee).then_inc(s.h, 16))
                toks.append((sem, sem.n))
            tok = toks if len(toks) > 1 else toks[0]
        else:
            sem = self.sems[eng]
            n = len(fns)
            for i, fn in enumerate(fns):
                if i == n - 1:
                    sem.n += 1
                    self.ops[eng].append(
                        lambda ee, f=fn, s=sem: f(ee).then_inc(s.h, 1))
                else:
                    self.ops[eng].append(lambda ee, f=fn: f(ee))
            tok = (sem, sem.n)
        for b in reads:
            b.readers.append(tok)
        for b in writes:
            b.ready = tok
            b.readers = []
        return tok

    def raw_wait(self, eng, tok):
        self._wait(eng, tok)

    def barrier(self):
        toks = [(s, s.n) for s in self.all_sems() if s.n > 0]
        for e in ENGINES:
            own = self.sems[e]
            for t in toks:
                if t[0] is not own:
                    self._wait(e, t)


def host_layout(inputs):
    W_qkv = np.asarray(inputs["L0_attn_W_qkv"], np.float32)
    b_qkv = np.asarray(inputs["L0_attn_b_qkv"], np.float32)
    W_out = np.asarray(inputs["L0_attn_W_out"], np.float32)
    b_out = np.asarray(inputs["L0_attn_b_out"], np.float32)
    Wg = np.asarray(inputs["L0_ffn_Wg"], np.float32)
    bg = np.asarray(inputs["L0_ffn_bg"], np.float32)
    Wi = np.asarray(inputs["L0_ffn_Wi"], np.float32)
    Wo = np.asarray(inputs["L0_ffn_Wo"], np.float32)
    A = np.asarray(inputs["L0_ffn_A"], np.float32)
    B = np.asarray(inputs["L0_ffn_B"], np.float32)

    out = {}
    out["wiT"] = np.ascontiguousarray(Wi.T)
    out["woT"] = np.ascontiguousarray(Wo.T)
    out["wgA"] = np.ascontiguousarray(
        np.concatenate([Wg.T, A.reshape(E * RK, D).T], axis=1))
    out["bgA"] = np.ascontiguousarray(
        np.concatenate([bg, np.zeros(E * RK, np.float32)])[None, :])
    bblk = np.zeros((128, FF), np.float32)
    for e in range(E):
        for r in range(RK):
            bblk[32 * e + 4 * e + r, :] = B[e, :, r]
    out["bblk"] = bblk
    wq = np.zeros((NPAIR * D, 128), np.float32)
    wk = np.zeros((NPAIR * D, 128), np.float32)
    wv = np.zeros((NPAIR * D, 128), np.float32)
    bq = np.zeros((128, NPAIR), np.float32)
    bk = np.zeros((128, NPAIR), np.float32)
    bv = np.zeros((1, D), np.float32)
    for p_ in range(NPAIR):
        for j in range(2):
            h = 2 * p_ + j
            base = h * 3 * DH
            wq[p_ * D:(p_ + 1) * D, j * DH:(j + 1) * DH] = W_qkv[base:base + DH, :].T
            wk[p_ * D:(p_ + 1) * D, j * DH:(j + 1) * DH] = W_qkv[base + DH:base + 2 * DH, :].T
            wv[p_ * D:(p_ + 1) * D, j * DH:(j + 1) * DH] = W_qkv[base + 2 * DH:base + 3 * DH, :].T
            bq[j * DH:(j + 1) * DH, p_] = b_qkv[base:base + DH]
            bk[j * DH:(j + 1) * DH, p_] = b_qkv[base + DH:base + 2 * DH]
            bv[0, p_ * 128 + j * DH:p_ * 128 + (j + 1) * DH] = \
                b_qkv[base + 2 * DH:base + 3 * DH]
    out["wqp"], out["wkp"], out["wvp"] = wq, wk, wv
    out["bqp"], out["bkp"], out["bvr"] = bq, bk, bv
    out["woutT"] = np.ascontiguousarray(W_out.T)
    out["boutr"] = np.ascontiguousarray(b_out[None, :])
    out["ident"] = np.eye(128, dtype=np.float32)
    return out


def build_nc(S, n_phases=3):
    NT = S // 128
    NFC = FF // 512
    NQB = S // 512
    NKT = S // 128

    nc = bass.Bass("TRN2", target_bir_lowering=False)
    p = Prog()

    x_in = nc.dram_tensor("x", [S, D], F32, kind="ExternalInput").ap()
    y_out = nc.dram_tensor("y", [S, D], F32, kind="ExternalOutput").ap()
    x1_d = nc.dram_tensor("x1_scratch", [S, D], F32, kind="Internal").ap()
    x2_d = nc.dram_tensor("x2_scratch", [S, D], F32, kind="Internal").ap()
    xna_d = nc.dram_tensor("xna_scratch", [S, D], BF16, kind="Internal").ap()
    wiT_d = nc.dram_tensor("wiT", [D, FF], F32, kind="ExternalInput").ap()
    woT_d = nc.dram_tensor("woT", [FF, D], F32, kind="ExternalInput").ap()
    wgA_d = nc.dram_tensor("wgA", [D, 20], F32, kind="ExternalInput").ap()
    bgA_d = nc.dram_tensor("bgA", [1, 20], F32, kind="ExternalInput").ap()
    bblk_d = nc.dram_tensor("bblk", [128, FF], F32, kind="ExternalInput").ap()
    wqp_d = nc.dram_tensor("wqp", [NPAIR * D, 128], F32, kind="ExternalInput").ap()
    wkp_d = nc.dram_tensor("wkp", [NPAIR * D, 128], F32, kind="ExternalInput").ap()
    wvp_d = nc.dram_tensor("wvp", [NPAIR * D, 128], F32, kind="ExternalInput").ap()
    bqp_d = nc.dram_tensor("bqp", [128, NPAIR], F32, kind="ExternalInput").ap()
    bkp_d = nc.dram_tensor("bkp", [128, NPAIR], F32, kind="ExternalInput").ap()
    bvr_d = nc.dram_tensor("bvr", [1, D], F32, kind="ExternalInput").ap()
    woutT_d = nc.dram_tensor("woutT", [D, D], F32, kind="ExternalInput").ap()
    boutr_d = nc.dram_tensor("boutr", [1, D], F32, kind="ExternalInput").ap()
    ident_d = nc.dram_tensor("ident", [128, 128], F32, kind="ExternalInput").ap()

    ctx = ExitStack()
    sb = lambda name, shape, dt: ctx.enter_context(nc.sbuf_tensor(name, shape, dt))
    ps = lambda name, shape, dt: ctx.enter_context(nc.psum_tensor(name, shape, dt))

    NA = DCH * max(FF, S)
    aA = sb("aA", [128, NA], BF16)
    NB = max((FF // 128) * D, NPAIR * S)
    aB = sb("aB", [128, NB], BF16)
    C_xn = [0, 1024]
    C_xnT = [2048, 3072]
    C_hsum = [4096, 4608, 5120, 5632]
    C_hsumT = [6144, 6656, 7168, 7680]
    C_base = [8192, 8704]
    C_xnatt = [9216, 10240]
    C_bblk = 16384
    FFN_C = 20480
    C_QT = 0
    C_KT = 2 * S
    C_V = 4 * S
    C_PT = 4 * S + NKT * 132
    C_sq = C_PT + 2048
    C_raw = C_sq + 512
    C_V2 = C_raw + 512
    ATTN_C = C_V2 + NKT * 132
    aC = sb("aC", [128, max(FFN_C, ATTN_C)], BF16)

    xt_s = sb("xt", [128, 2 * D], F32)
    xo_s = sb("xo", [128, D], F32)
    mv_s = sb("mv", [128, 4], F32)
    bnst_s = sb("bnst", [128, 12], F32)
    rst_s = sb("rst", [128, 4], F32)
    nmr_s = sb("nmr", [128, 2], F32)
    epsc_s = sb("epsc", [128, 1], F32)
    gat_s = sb("gat", [128, 24], F32)
    w_s = sb("wcol", [128, 8], F32)
    llb_s = sb("llb", [128, 16], BF16)
    llT_s = sb("llT", [128, 128], BF16)
    wgA_s = sb("wgA_s", [128, DCH * 20], BF16)
    bgA_s = sb("bgA_s", [1, 20], BF16)
    ident_s = sb("ident_s", [128, 128], BF16)
    onesb_s = sb("onesb", [128, 128], BF16)
    wq_s = sb("wq_s", [128, DCH * 128], BF16)
    wk_s = sb("wk_s", [128, DCH * 128], BF16)
    wv_s = sb("wv_s", [128, DCH * 128], BF16)
    bq8_s = sb("bq8", [128, NPAIR], F32)
    bkp_s = sb("bkp_s", [128, NPAIR], F32)
    bvr_s = sb("bvr_s", [1, 128], BF16)
    boutr_s = sb("boutr_s", [1, D], BF16)
    sqm_s = sb("sqm", [1, 4 * NQB + 8], F32)
    negc_s = sb("negc", [128, 2], F32)
    dnl_s = sb("dnl", [1, 528], F32)
    dnr_s = sb("dnr", [1, 516], BF16)

    PALL = ps("pall", [128, 2048], F32)
    PA = [PALL[:, 0:1024], PALL[:, 1024:2048]]
    PC = [ps(f"pc{i}", [128, 512], F32) for i in range(3)]
    PT_ = ps("pbT", [128, 1024], BF16)

    for s in p.all_sems():
        s.h = ctx.enter_context(nc.semaphore(s.name))

    b_xt = [Buf("xt0"), Buf("xt1")]
    b_xo = [Buf("xo")]
    b_mv = [Buf("mv0"), Buf("mv1")]
    b_bnst = Buf("bnst")
    b_rst = [Buf("rst0"), Buf("rst1")]
    b_nmr = [Buf("nmr0"), Buf("nmr1")]
    b_xn = [Buf("xn0"), Buf("xn1")]
    b_xnT = [Buf("xnT0"), Buf("xnT1")]
    b_gat = Buf("gat")
    b_w = [Buf("w0"), Buf("w1")]
    b_llb = Buf("llb")
    b_llT = Buf("llT")
    b_basebf = [Buf("basebf0"), Buf("basebf1")]
    b_lorabf = Buf("lorabf")
    b_tmp = [Buf("tmp0"), Buf("tmp1")]
    b_hsum = [Buf(f"hsum{i}") for i in range(4)]
    b_hsumT = [Buf(f"hsumT{i}") for i in range(4)]
    b_xna = [Buf("xna0"), Buf("xna1")]
    b_xnat = [Buf(f"xnat{i}") for i in range(NT)]
    b_pa = [[Buf(f"pa{i}_{j}") for j in range(2)] for i in range(2)]
    b_pc = [Buf(f"pc{i}") for i in range(3)]
    b_pT = Buf("pbT")
    b_x1t = [Buf(f"x1t{i}") for i in range(NT)]
    b_x2t = [Buf(f"x2t{i}") for i in range(NT)]
    b_wiT = [Buf(f"wiT{j}") for j in range(DCH)]
    b_woT = [Buf(f"woT{j}") for j in range(FF // 128)]
    b_wsm = Buf("w_small")
    b_consts = Buf("consts")
    b_wq = Buf("wq")
    b_bvp = Buf("bvp")
    b_wk = Buf("wk")
    b_wv = Buf("wv")
    b_xnTa = [Buf(f"xnTall{i}") for i in range(NT)]
    b_QT = [Buf("QT0"), Buf("QT1")]
    b_KT = [Buf("KT0"), Buf("KT1")]
    b_V = [Buf("V0"), Buf("V1")]
    b_PT = [Buf(f"PT{i}") for i in range(4)]
    b_sq = [Buf("sq0"), Buf("sq1")]
    b_sqm = Buf("sqm")
    b_negc = Buf("negc")
    b_dnl = Buf("dnl")
    b_rdnb = [Buf("rdnb0"), Buf("rdnb1")]
    b_AT = Buf("AT")
    b_woutT = [Buf(f"woutT{j}") for j in range(DCH)]

    def fwiT(j):
        return aA[:, j * FF:(j + 1) * FF]

    def fwoT(j):
        return aB[:, j * D:(j + 1) * D]

    def axnT(j):
        return aA[:, j * S:(j + 1) * S]

    def awoutT(j):
        return aA[:, j * D:(j + 1) * D]

    def aAT(pp):
        return aB[:, pp * S:(pp + 1) * S]

    p.emit("vector", [
        lambda e: e.memset(onesb_s[:], 1.0),
        lambda e: e.memset(epsc_s[:], 1e-5),
    ], writes=[b_consts])

    for dst, src in [(wgA_s[:].rearrange("p (c f) -> p c f", c=DCH),
                      wgA_d.rearrange("(c p) f -> p c f", p=128)),
                     (bgA_s[:], bgA_d),
                     (ident_s[:], ident_d),
                     (boutr_s[:], boutr_d)]:
        p.emit("gpsimd", [lambda e, dst=dst, src=src: e.dma_start(out=dst, in_=src)],
               writes=[b_wsm], dma=True)
    p.emit("sync", [lambda e: e.dma_start(out=bq8_s[:], in_=bqp_d)],
           writes=[b_wsm], dma=True)
    p.emit("sync", [lambda e: e.dma_start(out=bkp_s[:], in_=bkp_d)],
           writes=[b_wsm], dma=True)
    p.emit("vector", [lambda e: e.tensor_scalar(
        bq8_s[:], bq8_s[:], 0.125, None, op0=ALU.mult)], reads=[b_wsm], writes=[b_wsm])

    def load_ffn_weights():
        p.emit("gpsimd", [lambda e: e.dma_start(
            out=aC[:, C_bblk:C_bblk + FF], in_=bblk_d)], writes=[b_wsm], dma=True)
        for j in range(DCH):
            p.emit("gpsimd", [lambda e, j=j: e.dma_start(
                out=fwiT(j), in_=wiT_d[j * 128:(j + 1) * 128, :])],
                writes=[b_wiT[j]], dma=True)
        for j in range(FF // 128):
            p.emit("gpsimd", [lambda e, j=j: e.dma_start(
                out=fwoT(j), in_=woT_d[j * 128:(j + 1) * 128, :])],
                writes=[b_woT[j]], dma=True)

    load_ffn_weights()

    def layernorm_tile(par, xt, xn_ap, b_xn_slot):
        mv = mv_s[:, par * 2:par * 2 + 2]
        lnv = rst_s[:, par * 2:par * 2 + 1]
        rstd = rst_s[:, par * 2 + 1:par * 2 + 2]
        nmr = nmr_s[:, par:par + 1]
        p.emit("vector", [
            lambda e: e.bn_stats(bnst_s[:, 0:6], xt[:, 0:512]),
            lambda e: e.bn_stats(bnst_s[:, 6:12], xt[:, 512:1024]),
        ], reads=[b_xt[par]], writes=[b_bnst])
        p.emit("vector", [lambda e: e.bn_aggr(mv, bnst_s[:])],
               reads=[b_bnst], writes=[b_mv[par]])
        p.emit("scalar", [lambda e: e.activation(lnv, mv[:, 1:2], AF.Ln, bias=epsc_s[:])],
               reads=[b_mv[par], b_consts], writes=[b_rst[par]])
        p.emit("scalar", [lambda e: e.activation(rstd, lnv, AF.Exp, scale=-0.5)],
               writes=[b_rst[par]])
        p.emit("vector", [lambda e: e.tensor_scalar(
            nmr, mv[:, 0:1], rstd, -1.0, op0=ALU.mult, op1=ALU.mult)],
            reads=[b_rst[par], b_mv[par]], writes=[b_nmr[par]])
        p.emit("scalar", [lambda e: e.activation(
            xn_ap, xt, AF.Identity, bias=nmr, scale=rstd)],
            reads=[b_xt[par], b_rst[par], b_nmr[par]], writes=[b_xn_slot])

    def transpose_xn(xn, dst_of_j, b_src, b_dst):
        p.emit("scalar", [
            (lambda e, j=j: e.dma_start(out=dst_of_j(j),
                                        in_=xn[:, j * 128:(j + 1) * 128],
                                        transpose=True))
            for j in range(0, DCH, 2)
        ], reads=[b_src], writes=[b_dst], dma=True)
        p.emit("sync", [
            (lambda e, j=j: e.dma_start(out=dst_of_j(j),
                                        in_=xn[:, j * 128:(j + 1) * 128],
                                        transpose=True))
            for j in range(1, DCH, 2)
        ], reads=[b_src], writes=[b_dst], dma=True)

    def layernorm_tile_on(par, xsrc, b_src, xn_ap, b_xn_slot):
        mv = mv_s[:, par * 2:par * 2 + 2]
        lnv = rst_s[:, par * 2:par * 2 + 1]
        rstd = rst_s[:, par * 2 + 1:par * 2 + 2]
        nmr = nmr_s[:, par:par + 1]
        p.emit("vector", [
            lambda e: e.bn_stats(bnst_s[:, 0:6], xsrc[:, 0:512]),
            lambda e: e.bn_stats(bnst_s[:, 6:12], xsrc[:, 512:1024]),
        ], reads=[b_src], writes=[b_bnst])
        p.emit("vector", [lambda e: e.bn_aggr(mv, bnst_s[:])],
               reads=[b_bnst], writes=[b_mv[par]])
        p.emit("scalar", [lambda e: e.activation(lnv, mv[:, 1:2], AF.Ln, bias=epsc_s[:])],
               reads=[b_mv[par], b_consts], writes=[b_rst[par]])
        p.emit("scalar", [lambda e: e.activation(rstd, lnv, AF.Exp, scale=-0.5)],
               writes=[b_rst[par]])
        p.emit("vector", [lambda e: e.tensor_scalar(
            nmr, mv[:, 0:1], rstd, -1.0, op0=ALU.mult, op1=ALU.mult)],
            reads=[b_rst[par], b_mv[par]], writes=[b_nmr[par]])
        p.emit("scalar", [lambda e: e.activation(
            xn_ap, xsrc, AF.Identity, bias=nmr, scale=rstd)],
            reads=[b_src, b_rst[par], b_nmr[par]], writes=[b_xn_slot])

    def ffn_phase(src_d, src_tiles, dst_d, dst_tiles, attn_prep=False):
        backlog = []

        def pop_backlog(n):
            for _ in range(min(n, len(backlog))):
                backlog.pop(0)()

        def load_tile(i):
            par = i % 2
            xt = xt_s[:, par * D:(par + 1) * D]
            if src_tiles is not None and src_tiles[i].ready is not None:
                p.raw_wait("sync", src_tiles[i].ready)
            p.emit("sync", [lambda e, i=i, xt=xt: e.dma_start(
                out=xt, in_=src_d[i * 128:(i + 1) * 128, :])],
                writes=[b_xt[par]], dma=True)

        def ln_tile(i):
            par = i % 2
            xt = xt_s[:, par * D:(par + 1) * D]
            xn = aC[:, C_xn[par]:C_xn[par] + 1024]
            xnT = aC[:, C_xnT[par]:C_xnT[par] + 1024]
            layernorm_tile(par, xt, xn, b_xn[par])
            transpose_xn(xn, lambda j, xnT=xnT: xnT[:, j * 128:(j + 1) * 128],
                         b_xn[par], b_xnT[par])

        for i in range(NT):
            par = i % 2
            xt = xt_s[:, par * D:(par + 1) * D]
            xo = xo_s[:, 0:D]
            xn = aC[:, C_xn[par]:C_xn[par] + 1024]
            xnT = aC[:, C_xnT[par]:C_xnT[par] + 1024]
            wc = w_s[:, par * 4:par * 4 + 4]

            if i == 0:
                load_tile(0)
                ln_tile(0)
                if NT > 1:
                    load_tile(1)
            if i + 1 < NT:
                ln_tile(i + 1)

            def base_mm(fc):
                p.emit("tensor", [
                    (lambda e, j=j, fc=fc, xnT=xnT: e.matmul(
                        PC[0][:, :], xnT[:, j * 128:(j + 1) * 128],
                        fwiT(j)[:, fc * 512:(fc + 1) * 512],
                        start=(j == 0), stop=(j == DCH - 1)))
                    for j in range(DCH)
                ], reads=[b_xnT[par]] + b_wiT, writes=[b_pc[0]])
                basebf = aC[:, C_base[fc % 2]:C_base[fc % 2] + 512]
                p.emit("scalar", [lambda e, basebf=basebf: e.copy(basebf, PC[0][:, :])],
                       reads=[b_pc[0]], writes=[b_basebf[fc % 2]])

            pg = PA[0][:, 0:20]
            mm = [(lambda e, j=j, xnT=xnT: e.matmul(
                pg, xnT[:, j * 128:(j + 1) * 128], wgA_s[:, j * 20:(j + 1) * 20],
                start=(j == 0), stop=False)) for j in range(DCH)]
            mm.append(lambda e: e.matmul(pg, onesb_s[0:1, :], bgA_s[:],
                                         start=False, stop=True))
            p.emit("tensor", mm, reads=[b_xnT[par], b_wsm, b_consts],
                   writes=[b_pa[0][0]])

            g = gat_s
            p.emit("vector", [lambda e: e.tensor_reduce(
                g[:, 4:5], PA[0][:, 0:4], AX.X, ALU.max, negate=True)],
                reads=[b_pa[0][0]], writes=[b_gat])
            p.emit("scalar", [lambda e: e.activation(
                g[:, 0:4], PA[0][:, 0:4], AF.Exp, bias=g[:, 4:5],
                accum_out=g[:, 5:6])], reads=[b_pa[0][0], b_gat], writes=[b_gat])
            p.emit("vector", [lambda e: e.reciprocal(g[:, 4:5], g[:, 5:6])],
                   writes=[b_gat])
            p.emit("vector", [
                lambda e: e.tensor_scalar(g[:, 4:5], g[:, 4:5], 0.5, None, op0=ALU.mult),
            ], writes=[b_gat])
            p.emit("vector", [
                lambda e: e.tensor_scalar(g[:, 6:10], g[:, 0:4], g[:, 4:5], None,
                                          op0=ALU.mult),
            ], writes=[b_gat])
            rank_fns = []
            for ee in range(E):
                if ee > 0:
                    rank_fns.append(lambda e, ee=ee: e.tensor_scalar(
                        g[:, 18:18 + ee], g[:, 6:6 + ee], g[:, 6 + ee:7 + ee], None,
                        op0=ALU.is_ge))
                    rank_fns.append(lambda e, ee=ee: e.tensor_reduce(
                        g[:, 10 + ee:11 + ee], g[:, 18:18 + ee], AX.X, ALU.add))
                if ee < E - 1:
                    rank_fns.append(lambda e, ee=ee: e.tensor_scalar(
                        g[:, 18:18 + (E - 1 - ee)], g[:, 7 + ee:10],
                        g[:, 6 + ee:7 + ee], None, op0=ALU.is_gt))
                    rank_fns.append(lambda e, ee=ee: e.tensor_reduce(
                        g[:, 22:23], g[:, 18:18 + (E - 1 - ee)], AX.X, ALU.add))
                    if ee > 0:
                        rank_fns.append(lambda e, ee=ee: e.tensor_tensor(
                            g[:, 10 + ee:11 + ee], g[:, 10 + ee:11 + ee],
                            g[:, 22:23], ALU.add))
                    else:
                        rank_fns.append(lambda e: e.tensor_copy(g[:, 10:11], g[:, 22:23]))
            rank_fns.append(lambda e: e.tensor_scalar(
                g[:, 14:18], g[:, 10:14], 1.5, None, op0=ALU.is_lt))
            rank_fns.append(lambda e, wc=wc: e.tensor_tensor(
                wc, g[:, 14:18], g[:, 6:10], ALU.mult))
            for fn in rank_fns:
                p.emit("vector", [fn], writes=[b_gat, b_w[par]])

            p.emit("scalar", [lambda e: e.copy(llb_s[:], PA[0][:, 4:20])],
                   reads=[b_pa[0][0]], writes=[b_llb])
            p.emit("tensor", [lambda e: e.transpose(
                PT_[0:16, 512:640], llb_s[:], ident_s[:])],
                reads=[b_llb, b_wsm], writes=[b_pT])
            p.emit("scalar", [
                (lambda e, ee=ee: e.copy(llT_s[32 * ee:32 * ee + 16, :],
                                         PT_[0:16, 512:640]))
                for ee in range(E)
            ], reads=[b_pT], writes=[b_llT])

            for fc in range(NFC):
                fpar = fc % 2
                f4 = fc % 4
                basebf = aC[:, C_base[fpar]:C_base[fpar] + 512]
                hsum = aC[:, C_hsum[f4]:C_hsum[f4] + 512]
                hsumT = aC[:, C_hsumT[f4]:C_hsumT[f4] + 512]
                base_mm(fc)
                for ee in range(E):
                    p.emit("tensor", [lambda e, ee=ee, fc=fc: e.matmul(
                        PA[ee // 2][:, (ee % 2) * 512:(ee % 2) * 512 + 512],
                        llT_s[32 * ee:32 * ee + 16, :],
                        aC[32 * ee:32 * ee + 16, C_bblk + fc * 512:C_bblk + (fc + 1) * 512],
                        start=True, stop=True, tile_position=(32 * ee, 0))],
                        reads=[b_llT, b_wsm], writes=[b_pa[ee // 2][ee % 2]])
                pop_backlog(3 if fc == 0 else 2)
                if fc == 2 and i + 1 < NT:
                    ln_tile(i + 1)
                tl = 12288 + fpar * 2048
                tlb = aC[:, tl:tl + 2048]
                bl = b_tmp[fpar]
                p.emit("vector", [
                    lambda e, tlb=tlb, basebf=basebf: e.tensor_tensor(
                        tlb.rearrange("p (g f) -> p g f", g=E),
                        PALL[:, :].rearrange("p (g f) -> p g f", g=E),
                        basebf.rearrange("p (g f) -> p g f", g=1).broadcast_to(
                            [128, E, 512]),
                        ALU.add)
                ], reads=[b_basebf[fpar], b_pa[0][0], b_pa[0][1], b_pa[1][0],
                          b_pa[1][1]], writes=[bl])
                p.emit("vector", [
                    (lambda e, ee=ee, tlb=tlb, wc=wc: e.tensor_scalar(
                        tlb[:, ee * 512:(ee + 1) * 512],
                        tlb[:, ee * 512:(ee + 1) * 512],
                        wc[:, ee:ee + 1], 0.0, op0=ALU.mult, op1=ALU.max))
                    for ee in range(E)
                ], reads=[b_w[par]], writes=[bl])
                p.emit("vector", [
                    lambda e, tlb=tlb: e.tensor_tensor(
                        tlb[:, 0:512], tlb[:, 0:512], tlb[:, 512:1024], ALU.add),
                    lambda e, tlb=tlb: e.tensor_tensor(
                        tlb[:, 1024:1536], tlb[:, 1024:1536], tlb[:, 1536:2048],
                        ALU.add),
                ], writes=[bl])
                p.emit("vector", [lambda e, tlb=tlb, hsum=hsum: e.tensor_tensor(
                    hsum, tlb[:, 0:512], tlb[:, 1024:1536], ALU.add)],
                    reads=[bl], writes=[b_hsum[f4]])

                def defer_tail(fc=fc, f4=f4, hsum=hsum, hsumT=hsumT):
                    def t1():
                        p.emit("tensor", [
                            (lambda e, q=q: e.transpose(
                                PT_[:, (f4 % 2) * 512 + q * 128:
                                    (f4 % 2) * 512 + (q + 1) * 128],
                                hsum[:, q * 128:(q + 1) * 128], ident_s[:]))
                            for q in range(4)
                        ], reads=[b_hsum[f4], b_wsm], writes=[b_pT])
                        p.emit("scalar", [lambda e: e.copy(
                            hsumT, PT_[:, (f4 % 2) * 512:(f4 % 2) * 512 + 512])],
                            reads=[b_pT], writes=[b_hsumT[f4]])

                    def t2():
                        p.emit("tensor", [
                            (lambda e, q=q, dm=dm: e.matmul(
                                PC[1 + dm][:, :],
                                hsumT[:, q * 128:(q + 1) * 128],
                                fwoT(fc * 4 + q)[:, dm * 512:(dm + 1) * 512],
                                start=(fc == 0 and q == 0),
                                stop=(fc == NFC - 1 and q == 3)))
                            for dm in range(2) for q in range(4)
                        ], reads=[b_hsumT[f4]] + b_woT,
                            writes=[b_pc[1], b_pc[2]])
                    return [t1, t2]

                backlog.extend(defer_tail())

            def epilogue(i=i, par=par, xo=xo, xt=xt):
                p.emit("vector", [
                    lambda e: e.tensor_tensor(
                        xo[:, 0:512], PC[1][:, :], xt[:, 0:512], ALU.add),
                    lambda e: e.tensor_tensor(
                        xo[:, 512:1024], PC[2][:, :], xt[:, 512:1024], ALU.add),
                ], reads=[b_pc[1], b_pc[2], b_xt[par]], writes=[b_xo[0]])
                p.emit("gpsimd", [lambda e: e.dma_start(
                    out=dst_d[i * 128:(i + 1) * 128, :], in_=xo)],
                    reads=[b_xo[0]],
                    writes=([dst_tiles[i]] if dst_tiles is not None else []),
                    dma=True)
                if attn_prep:
                    xna = aC[:, C_xnatt[par]:C_xnatt[par] + 1024]
                    layernorm_tile_on(par, xo, b_xo[0], xna, b_xna[par])
                    p.emit("gpsimd", [lambda e: e.dma_start(
                        out=xna_d[i * 128:(i + 1) * 128, :], in_=xna)],
                        reads=[b_xna[par]], writes=[b_xnat[i]], dma=True)
                if i + 2 < NT:
                    load_tile(i + 2)

            backlog.append(epilogue)
        pop_backlog(len(backlog))

    ffn_phase(x_in, None, x1_d, b_x1t, attn_prep=(n_phases > 1))
    if n_phases == 1:
        finish(nc, p, ctx)
        return nc
    p.barrier()

    for i in range(NT):
        p.raw_wait("scalar", b_xnat[i].ready)
        p.raw_wait("sync", b_xnat[i].ready)
        p.emit("scalar", [
            (lambda e, j=j, i=i: e.dma_start(
                out=axnT(j)[:, i * 128:(i + 1) * 128],
                in_=xna_d[i * 128:(i + 1) * 128, j * 128:(j + 1) * 128],
                transpose=True))
            for j in range(0, DCH, 2)
        ], writes=[b_xnTa[i]], dma=True)
        p.emit("sync", [
            (lambda e, j=j, i=i: e.dma_start(
                out=axnT(j)[:, i * 128:(i + 1) * 128],
                in_=xna_d[i * 128:(i + 1) * 128, j * 128:(j + 1) * 128],
                transpose=True))
            for j in range(1, DCH, 2)
        ], writes=[b_xnTa[i]], dma=True)

    if ATTN_STAGE == 1:
        finish(nc, p, ctx)
        return nc

    p.emit("vector", [lambda e: e.memset(aC[:, C_V:C_V + NKT * 132], 1.0)],
           writes=[b_V[0]])
    p.emit("vector", [lambda e: e.memset(aC[:, C_V2:C_V2 + NKT * 132], 1.0)],
           writes=[b_V[1]])

    def vbase(pp):
        return C_V if pp % 2 == 0 else C_V2

    def QTb(par):
        return aC[:, C_QT + par * S:C_QT + (par + 1) * S]

    def KTb(par):
        return aC[:, C_KT + par * S:C_KT + (par + 1) * S]

    def load_pair_w(pp):
        p.emit("gpsimd", [lambda e, pp=pp: e.dma_start(
            out=bvr_s[:], in_=bvr_d[0:1, pp * 128:(pp + 1) * 128])],
            writes=[b_bvp], dma=True)
        for wdst, wsrc, bb in [(wq_s, wqp_d, b_wq), (wk_s, wkp_d, b_wk),
                               (wv_s, wvp_d, b_wv)]:
            p.emit("gpsimd", [lambda e, wdst=wdst, wsrc=wsrc, pp=pp: e.dma_start(
                out=wdst[:].rearrange("p (c m) -> p c m", c=DCH),
                in_=wsrc[pp * D:(pp + 1) * D, :].rearrange("(c p) m -> p c m", p=128))],
                writes=[bb], dma=True)

    def qkv_group(pp, kind, qb, bank_ap, bank_buf):
        par = pp % 2
        wsrc = wq_s if kind == 0 else wk_s
        dst = QTb(par) if kind == 0 else KTb(par)
        bdst = b_QT[par] if kind == 0 else b_KT[par]
        bwsrc = b_wq if kind == 0 else b_wk
        bcol = bq8_s[:, pp:pp + 1] if kind == 0 else bkp_s[:, pp:pp + 1]
        scl = 0.125 if kind == 0 else 1.0
        tiles = [b_xnTa[t] for t in range(qb * 4, qb * 4 + 4)]
        p.emit("tensor", [
            (lambda e, j=j, qb=qb, wsrc=wsrc, bank_ap=bank_ap: e.matmul(
                bank_ap, wsrc[:, j * 128:(j + 1) * 128],
                axnT(j)[:, qb * 512:(qb + 1) * 512],
                start=(j == 0), stop=(j == DCH - 1)))
            for j in range(DCH)
        ], reads=[bwsrc] + tiles, writes=[bank_buf])
        p.emit("vector", [lambda e, qb=qb, dst=dst, bank_ap=bank_ap, bcol=bcol,
               scl=scl: e.tensor_scalar(
                   dst[:, qb * 512:(qb + 1) * 512], bank_ap,
                   scl, bcol, op0=ALU.mult, op1=ALU.add)],
               reads=[bank_buf, b_wsm], writes=[bdst])

    def v_group(pp, kt):
        mm = [(lambda e, j=j, kt=kt: e.matmul(
            PC[2][:, 0:128], axnT(j)[:, kt * 128:(kt + 1) * 128],
            wv_s[:, j * 128:(j + 1) * 128],
            start=(j == 0), stop=False)) for j in range(DCH)]
        mm.append(lambda e: e.matmul(
            PC[2][:, 0:128], onesb_s[0:1, :], bvr_s[0:1, :],
            start=False, stop=True))
        p.emit("tensor", mm, reads=[b_wv, b_xnTa[kt], b_consts, b_bvp],
               writes=[b_pc[2]])
        vb = vbase(pp) + kt * 132
        p.emit("vector", [
            lambda e, vb=vb: e.tensor_copy(aC[:, vb:vb + 64], PC[2][:, 0:64]),
            lambda e, vb=vb: e.tensor_copy(aC[:, vb + 66:vb + 130],
                                           PC[2][:, 64:128]),
        ], reads=[b_pc[2]], writes=[b_V[pp % 2]])

    def v_phase(pp):
        for kt in range(NKT):
            v_group(pp, kt)

    def stab_group(pp, kind, qb):
        par = pp % 2
        src2 = QTb(par) if kind == 0 else KTb(par)
        bsrc = b_QT[par] if kind == 0 else b_KT[par]
        sq = aC[:, C_sq:C_sq + 512]
        p.emit("scalar", [lambda e, src2=src2, sq=sq, qb=qb: e.activation(
            sq, src2[:, qb * 512:(qb + 1) * 512], AF.Square)],
            reads=[bsrc], writes=[b_sq[0]])
        for half in range(2):
            npar = (qb * 2 + half) % 2
            nbank = PC[2][0:1, 0:512] if npar == 0 else PA[1][0:1, 0:512]
            nbuf = b_pc[2] if npar == 0 else b_pa[1][0]
            p.emit("tensor", [lambda e, half=half, sq=sq, nbank=nbank:
                   e.matmul(
                nbank, onesb_s[half * 64:half * 64 + 64, 0:1],
                sq[half * 64:half * 64 + 64, :],
                start=True, stop=True, tile_position=(half * 64, 0))],
                reads=[b_sq[0], b_consts], writes=[nbuf])
            off = 0 if half == 0 else 2 * NQB + 4
            p.emit("vector", [lambda e, off=off, kind=kind, qb=qb,
                   nbank=nbank: e.tensor_reduce(
                       sqm_s[0:1, off + kind * NQB + qb:
                             off + kind * NQB + qb + 1],
                       nbank, AX.X, ALU.max)],
                   reads=[nbuf], writes=[b_sqm])

    def stab_phase(pp):
        for kind in range(2):
            for qb in range(NQB):
                stab_group(pp, kind, qb)
        stab_finale(pp)

    def stab_finale(pp):
        for half in range(2):
            off = 0 if half == 0 else 2 * NQB + 4
            for kind in range(2):
                p.emit("vector", [lambda e, off=off, kind=kind, half=half:
                       e.tensor_reduce(
                           sqm_s[0:1, 2 * NQB + half * 2 + kind:
                                 2 * NQB + half * 2 + kind + 1],
                           sqm_s[0:1, off + kind * NQB:off + (kind + 1) * NQB],
                           AX.X, ALU.max)],
                       reads=[b_sqm], writes=[b_sqm])
            p.emit("vector", [lambda e, half=half: e.tensor_tensor(
                dnl_s[0:1, half:half + 1],
                sqm_s[0:1, 2 * NQB + half * 2:2 * NQB + half * 2 + 1],
                sqm_s[0:1, 2 * NQB + half * 2 + 1:2 * NQB + half * 2 + 2],
                ALU.mult)], reads=[b_sqm], writes=[b_dnl])
        p.emit("scalar", [lambda e: e.activation(dnl_s[0:1, 2:4], dnl_s[0:1, 0:2],
                                                 AF.Ln)],
               reads=[b_dnl], writes=[b_dnl])
        p.emit("scalar", [lambda e: e.activation(dnl_s[0:1, 4:6], dnl_s[0:1, 2:4],
                                                 AF.Exp, scale=0.5)],
               writes=[b_dnl])
        p.emit("vector", [lambda e: e.tensor_scalar(
            dnr_s[0:1, 512:514], dnl_s[0:1, 4:6], -1.0, None, op0=ALU.mult)],
            reads=[b_dnl], writes=[b_dnl])
        p.emit("tensor", [lambda e: e.matmul(
            PC[2][0:128, 0:2], onesb_s[0:1, :], dnr_s[0:1, 512:514],
            start=True, stop=True)],
            reads=[b_dnl, b_consts], writes=[b_pc[2]])
        p.emit("scalar", [lambda e: e.copy(negc_s[:], PC[2][:, 0:2])],
               reads=[b_pc[2]], writes=[b_negc])

    load_pair_w(0)
    for kind in range(2):
        for qb in range(NQB):
            qkv_group(0, kind, qb, PC[qb % 2][:, :], b_pc[qb % 2])
    v_phase(0)
    stab_phase(0)

    for pp in range(NPAIR):
        par = pp % 2
        QT, KT = QTb(par), KTb(par)
        if pp + 1 < NPAIR:
            load_pair_w(pp + 1)
        qkv_backlog = []
        if pp + 1 < NPAIR:
            for kind in range(2):
                for qb in range(NQB):
                    qkv_backlog.append(
                        lambda kind=kind, qb=qb: qkv_group(
                            pp + 1, kind, qb, PC[2][:, :], b_pc[2]))
            for kt in range(NKT):
                qkv_backlog.append(lambda kt=kt: v_group(pp + 1, kt))
            for kind in range(2):
                for qb in range(NQB):
                    qkv_backlog.append(
                        lambda kind=kind, qb=qb: stab_group(pp + 1, kind, qb))

        if ATTN_STAGE == 2:
            continue
        if ATTN_STAGE == 3:
            continue
        for qb in range(NQB):
            abl = []
            for kt in range(NKT):
                sub = kt % 2
                for half in range(2):
                    p.emit("tensor", [lambda e, half=half, kt=kt, qb=qb, sub=sub,
                           QT=QT, KT=KT: e.matmul(
                               PA[half][:, sub * 512:sub * 512 + 512],
                               KT[half * 64:half * 64 + 64,
                                  kt * 128:(kt + 1) * 128],
                               QT[half * 64:half * 64 + 64,
                                  qb * 512:(qb + 1) * 512],
                               start=True, stop=True,
                               tile_position=(half * 64, 0))],
                           reads=[b_QT[par], b_KT[par]],
                           writes=[b_pa[half][sub]])
                for half in range(2):
                    ptb = C_PT + (half * 2 + sub) * 512
                    p.emit("scalar", [lambda e, half=half, ptb=ptb, sub=sub:
                           e.activation(
                        aC[:, ptb:ptb + 512], PA[half][:, sub * 512:sub * 512 + 512],
                        AF.Exp, bias=negc_s[:, half:half + 1])],
                        reads=[b_pa[half][sub], b_negc],
                        writes=[b_PT[half * 2 + sub]])

                    def av(kt=kt, half=half, ptb=ptb, sub=sub, qb=qb):
                        ab = PC[half]
                        bab = b_pc[half]
                        vb = vbase(pp) + kt * 132 + half * 66
                        p.emit("tensor", [lambda e, ab=ab, vb=vb, ptb=ptb, kt=kt:
                               e.matmul(
                                   ab[0:65, :], aC[:, vb:vb + 65],
                                   aC[:, ptb:ptb + 512],
                                   start=(kt == 0), stop=(kt == NKT - 1))],
                               reads=[b_PT[half * 2 + sub], b_V[pp % 2]],
                               writes=[bab])
                    abl.append(av)
                if kt >= 1:
                    abl.pop(0)()
                    abl.pop(0)()
                if qkv_backlog and kt % 2 == 1:
                    qkv_backlog.pop(0)()
            while abl:
                abl.pop(0)()

            for half in range(2):
                ab = PC[half]
                bab = b_pc[half]
                rpar = (qb * 2 + half) % 2
                rawb = aC[rpar * 64:rpar * 64 + 64, C_raw:C_raw + 512]
                p.emit("scalar", [lambda e, ab=ab: e.activation(
                    dnl_s[0:1, 16:528], ab[64:65, :], AF.Ln)],
                    reads=[bab], writes=[b_dnl])
                p.emit("vector", [lambda e, ab=ab, rawb=rawb: e.tensor_copy(
                    rawb, ab[0:64, :])],
                    reads=[bab], writes=[b_rdnb[rpar]])
                p.emit("scalar", [lambda e: e.activation(
                    dnr_s[0:1, 0:512], dnl_s[0:1, 16:528], AF.Exp, scale=-1.0)],
                    writes=[b_dnl])
                p.emit("tensor", [lambda e, ab=ab: e.matmul(
                    ab[0:128, :], onesb_s[0:1, :], dnr_s[0:1, 0:512],
                    start=True, stop=True)],
                    reads=[b_dnl, b_consts], writes=[bab])
                p.emit("vector", [lambda e, ab=ab, rawb=rawb, pp=pp, half=half,
                       qb=qb: e.tensor_tensor(
                           aAT(pp)[half * 64:half * 64 + 64,
                                   qb * 512:(qb + 1) * 512],
                           rawb, ab[0:64, :], ALU.mult)],
                       reads=[bab], writes=[b_AT])
        while qkv_backlog:
            qkv_backlog.pop(0)()
        if pp + 1 < NPAIR:
            stab_finale(pp + 1)

    for j in range(DCH):
        p.emit("gpsimd", [lambda e, j=j: e.dma_start(
            out=awoutT(j), in_=woutT_d[j * 128:(j + 1) * 128, :])],
            writes=[b_woutT[j]] + b_xnTa, dma=True)
    for i in range(NT):
        par = i % 2
        xt = xt_s[:, par * D:(par + 1) * D]
        xo = xo_s[:, 0:D]
        if par == 0:
            banks = [(PC[0][:, :], b_pc[0]), (PC[1][:, :], b_pc[1])]
        else:
            banks = [(PA[0][:, 0:512], b_pa[0][0]), (PA[0][:, 512:1024], b_pa[0][1])]
        p.emit("sync", [lambda e, i=i, xt=xt: e.dma_start(
            out=xt, in_=x1_d[i * 128:(i + 1) * 128, :])],
            writes=[b_xt[par]], dma=True)
        for dm in range(2):
            pbk, bbk = banks[dm]
            mm = [(lambda e, ppx=ppx, i=i, dm=dm, pbk=pbk: e.matmul(
                pbk, aAT(ppx)[:, i * 128:(i + 1) * 128],
                awoutT(ppx)[:, dm * 512:(dm + 1) * 512],
                start=(ppx == 0), stop=False)) for ppx in range(NPAIR)]
            mm.append(lambda e, dm=dm, pbk=pbk: e.matmul(
                pbk, onesb_s[0:1, :], boutr_s[:, dm * 512:(dm + 1) * 512],
                start=False, stop=True))
            p.emit("tensor", mm,
                   reads=[b_AT, b_consts, b_wsm] + b_woutT, writes=[bbk])
        p.emit("vector", [
            lambda e, xo=xo, xt=xt, banks=banks: e.tensor_tensor(
                xo[:, 0:512], banks[0][0], xt[:, 0:512], ALU.add),
            lambda e, xo=xo, xt=xt, banks=banks: e.tensor_tensor(
                xo[:, 512:1024], banks[1][0], xt[:, 512:1024], ALU.add),
        ], reads=[banks[0][1], banks[1][1], b_xt[par]], writes=[b_xo[0]])
        p.emit("sync", [lambda e, i=i, xo=xo: e.dma_start(
            out=x2_d[i * 128:(i + 1) * 128, :], in_=xo)],
            reads=[b_xo[0]], writes=[b_x2t[i]], dma=True)

    if n_phases == 2:
        finish(nc, p, ctx)
        return nc

    def toks_of(bufs):
        out = []
        for b in bufs:
            if b.ready is not None:
                out.append(b.ready)
            out.extend(b.readers)
        return out

    attn_toks = toks_of(
        b_QT + b_KT + b_V + [b_sq[0], b_AT, b_negc, b_sqm, b_dnl] + b_PT + b_rdnb)
    at_toks = toks_of([b_AT] + b_woutT)
    xnta_toks = toks_of(b_xnTa)
    for b in (b_xn + b_xnT + b_hsum + b_hsumT + b_basebf + b_tmp + b_xna + [b_wsm]):
        b.readers.extend(attn_toks)
    for j in range(DCH):
        b_wiT[j].readers.extend(xnta_toks)
        if j < 2:
            b_wiT[j].readers.extend(at_toks)
    for j in range(FF // 128):
        b_woT[j].readers.extend(at_toks)

    load_ffn_weights()
    ffn_phase(x2_d, b_x2t, y_out, None)

    finish(nc, p, ctx)
    return nc


def finish(nc, p, ctx):
    for q in ("sync", "scalar", "gpsimd"):
        for s in p.dma_pools[q]:
            if s.n > 0:
                p.raw_wait("sync", (s, s.n))
    with nc.Block() as block:
        @block.sync
        def _(e):
            for f in p.ops["sync"]:
                f(e)

        @block.tensor
        def _(e):
            for f in p.ops["tensor"]:
                f(e)

        @block.vector
        def _(e):
            for f in p.ops["vector"]:
                f(e)

        @block.scalar
        def _(e):
            for f in p.ops["scalar"]:
                f(e)

        @block.gpsimd
        def _(e):
            for f in p.ops["gpsimd"]:
                f(e)
    ctx.close()


def kernel(**inputs):
    S = 4096
    x = np.asarray(inputs["L0_attn_x"], np.float32)
    lay = host_layout(inputs)
    nc = build_nc(S)
    in_maps = []
    for c in range(NCORES):
        m = {"x": np.ascontiguousarray(x[c])}
        m.update(lay)
        in_maps.append(m)
    res = run_bass_kernel_spmd(nc, in_maps, core_ids=list(range(NCORES)))
    out = np.stack([res.results[c]["y"] for c in range(NCORES)], axis=0)
    return out.astype(np.float32)


# version 31
# speedup vs baseline: 1.0275x; 1.0275x over previous
import numpy as np
import ml_dtypes
from contextlib import ExitStack

import concourse.bass as bass
import concourse.mybir as mybir
from concourse.bass_utils import run_bass_kernel_spmd

F32 = mybir.dt.float32
BF16 = mybir.dt.bfloat16
AF = mybir.ActivationFunctionType
ALU = mybir.AluOpType
AX = mybir.AxisListType

D = 1024
FF = 4096
E = 4
RK = 4
DH = 64
NPAIR = 8
NCORES = 8
DCH = D // 128

ENGINES = ("sync", "tensor", "vector", "scalar", "gpsimd")
ATTN_STAGE = 0
LOOP_PART = 4


class Sem:
    def __init__(self, name, step):
        self.name, self.step, self.n, self.h = name, step, 0, None


class Buf:
    __slots__ = ("name", "ready", "readers")

    def __init__(self, name):
        self.name, self.ready, self.readers = name, None, []


NDMASEM = 8


class Prog:
    def __init__(self):
        self.ops = {e: [] for e in ENGINES}
        self.sems = {e: Sem("c_" + e, 1) for e in ENGINES}
        self.dma_pools = {q: [Sem(f"d_{q}{i}", 16) for i in range(NDMASEM)]
                          for q in ("sync", "scalar", "gpsimd")}
        self.dma_idx = {q: 0 for q in self.dma_pools}
        self.waited = {e: {} for e in ENGINES}

    def all_sems(self):
        out = list(self.sems.values())
        for pool in self.dma_pools.values():
            out.extend(pool)
        return out

    def _wait(self, eng, tok):
        if tok is None:
            return
        if isinstance(tok, list):
            for t in tok:
                self._wait(eng, t)
            return
        sem, val = tok
        if self.waited[eng].get(sem.name, 0) >= val:
            return
        self.waited[eng][sem.name] = val
        self.ops[eng].append(lambda ee, s=sem, v=val: ee.wait_ge(s.h, v))

    def emit(self, eng, fns, reads=(), writes=(), dma=False):
        for b in reads:
            self._wait(eng, b.ready)
        for b in writes:
            self._wait(eng, b.ready)
            for t in b.readers:
                self._wait(eng, t)
        if dma:
            toks = []
            for fn in fns:
                pool = self.dma_pools[eng]
                sem = pool[self.dma_idx[eng]]
                self.dma_idx[eng] = (self.dma_idx[eng] + 1) % NDMASEM
                if sem.n > 0:
                    self._wait(eng, (sem, sem.n))
                sem.n += 16
                self.ops[eng].append(
                    lambda ee, f=fn, s=sem: f(ee).then_inc(s.h, 16))
                toks.append((sem, sem.n))
            tok = toks if len(toks) > 1 else toks[0]
        else:
            sem = self.sems[eng]
            n = len(fns)
            for i, fn in enumerate(fns):
                if i == n - 1:
                    sem.n += 1
                    self.ops[eng].append(
                        lambda ee, f=fn, s=sem: f(ee).then_inc(s.h, 1))
                else:
                    self.ops[eng].append(lambda ee, f=fn: f(ee))
            tok = (sem, sem.n)
        for b in reads:
            b.readers.append(tok)
        for b in writes:
            b.ready = tok
            b.readers = []
        return tok

    def raw_wait(self, eng, tok):
        self._wait(eng, tok)

    def barrier(self):
        toks = [(s, s.n) for s in self.all_sems() if s.n > 0]
        for e in ENGINES:
            own = self.sems[e]
            for t in toks:
                if t[0] is not own:
                    self._wait(e, t)


def host_layout(inputs):
    W_qkv = np.asarray(inputs["L0_attn_W_qkv"], np.float32)
    b_qkv = np.asarray(inputs["L0_attn_b_qkv"], np.float32)
    W_out = np.asarray(inputs["L0_attn_W_out"], np.float32)
    b_out = np.asarray(inputs["L0_attn_b_out"], np.float32)
    Wg = np.asarray(inputs["L0_ffn_Wg"], np.float32)
    bg = np.asarray(inputs["L0_ffn_bg"], np.float32)
    Wi = np.asarray(inputs["L0_ffn_Wi"], np.float32)
    Wo = np.asarray(inputs["L0_ffn_Wo"], np.float32)
    A = np.asarray(inputs["L0_ffn_A"], np.float32)
    B = np.asarray(inputs["L0_ffn_B"], np.float32)

    out = {}
    out["wiT"] = np.ascontiguousarray(Wi.T)
    out["woT"] = np.ascontiguousarray(Wo.T)
    out["wgA"] = np.ascontiguousarray(
        np.concatenate([Wg.T, A.reshape(E * RK, D).T], axis=1))
    out["bgA"] = np.ascontiguousarray(
        np.concatenate([bg, np.zeros(E * RK, np.float32)])[None, :])
    bblk = np.zeros((128, FF), np.float32)
    for e in range(E):
        for r in range(RK):
            bblk[32 * e + 4 * e + r, :] = B[e, :, r]
    out["bblk"] = bblk
    wq = np.zeros((NPAIR * D, 128), np.float32)
    wk = np.zeros((NPAIR * D, 128), np.float32)
    wv = np.zeros((NPAIR * D, 128), np.float32)
    bq = np.zeros((128, NPAIR), np.float32)
    bk = np.zeros((128, NPAIR), np.float32)
    bv = np.zeros((1, D), np.float32)
    for p_ in range(NPAIR):
        for j in range(2):
            h = 2 * p_ + j
            base = h * 3 * DH
            wq[p_ * D:(p_ + 1) * D, j * DH:(j + 1) * DH] = W_qkv[base:base + DH, :].T
            wk[p_ * D:(p_ + 1) * D, j * DH:(j + 1) * DH] = W_qkv[base + DH:base + 2 * DH, :].T
            wv[p_ * D:(p_ + 1) * D, j * DH:(j + 1) * DH] = W_qkv[base + 2 * DH:base + 3 * DH, :].T
            bq[j * DH:(j + 1) * DH, p_] = b_qkv[base:base + DH]
            bk[j * DH:(j + 1) * DH, p_] = b_qkv[base + DH:base + 2 * DH]
            bv[0, p_ * 128 + j * DH:p_ * 128 + (j + 1) * DH] = \
                b_qkv[base + 2 * DH:base + 3 * DH]
    out["wqp"], out["wkp"], out["wvp"] = wq, wk, wv
    out["bqp"], out["bkp"], out["bvr"] = bq, bk, bv
    out["woutT"] = np.ascontiguousarray(W_out.T)
    out["boutr"] = np.ascontiguousarray(b_out[None, :])
    out["ident"] = np.eye(128, dtype=np.float32)
    return out


def build_nc(S, n_phases=3):
    NT = S // 128
    NFC = FF // 512
    NQB = S // 512
    NKT = S // 128

    nc = bass.Bass("TRN2", target_bir_lowering=False)
    p = Prog()

    x_in = nc.dram_tensor("x", [S, D], F32, kind="ExternalInput").ap()
    y_out = nc.dram_tensor("y", [S, D], F32, kind="ExternalOutput").ap()
    x1_d = nc.dram_tensor("x1_scratch", [S, D], F32, kind="Internal").ap()
    x2_d = nc.dram_tensor("x2_scratch", [S, D], F32, kind="Internal").ap()
    xna_d = nc.dram_tensor("xna_scratch", [S, D], BF16, kind="Internal").ap()
    wiT_d = nc.dram_tensor("wiT", [D, FF], F32, kind="ExternalInput").ap()
    woT_d = nc.dram_tensor("woT", [FF, D], F32, kind="ExternalInput").ap()
    wgA_d = nc.dram_tensor("wgA", [D, 20], F32, kind="ExternalInput").ap()
    bgA_d = nc.dram_tensor("bgA", [1, 20], F32, kind="ExternalInput").ap()
    bblk_d = nc.dram_tensor("bblk", [128, FF], F32, kind="ExternalInput").ap()
    wqp_d = nc.dram_tensor("wqp", [NPAIR * D, 128], F32, kind="ExternalInput").ap()
    wkp_d = nc.dram_tensor("wkp", [NPAIR * D, 128], F32, kind="ExternalInput").ap()
    wvp_d = nc.dram_tensor("wvp", [NPAIR * D, 128], F32, kind="ExternalInput").ap()
    bqp_d = nc.dram_tensor("bqp", [128, NPAIR], F32, kind="ExternalInput").ap()
    bkp_d = nc.dram_tensor("bkp", [128, NPAIR], F32, kind="ExternalInput").ap()
    bvr_d = nc.dram_tensor("bvr", [1, D], F32, kind="ExternalInput").ap()
    woutT_d = nc.dram_tensor("woutT", [D, D], F32, kind="ExternalInput").ap()
    boutr_d = nc.dram_tensor("boutr", [1, D], F32, kind="ExternalInput").ap()
    ident_d = nc.dram_tensor("ident", [128, 128], F32, kind="ExternalInput").ap()

    ctx = ExitStack()
    sb = lambda name, shape, dt: ctx.enter_context(nc.sbuf_tensor(name, shape, dt))
    ps = lambda name, shape, dt: ctx.enter_context(nc.psum_tensor(name, shape, dt))

    NA = DCH * max(FF, S)
    aA = sb("aA", [128, NA], BF16)
    NB = max((FF // 128) * D, NPAIR * S)
    aB = sb("aB", [128, NB], BF16)
    C_xn = [0, 1024]
    C_xnT = [2048, 3072]
    C_hsum = [4096, 4608, 5120, 5632]
    C_hsumT = [6144, 6656, 7168, 7680]
    C_base = [8192, 8704]
    C_xnatt = [9216, 10240]
    C_bblk = 16384
    FFN_C = 20480
    C_QT = 0
    C_KT = 2 * S
    C_V = 4 * S
    C_PT = 4 * S + NKT * 132
    C_sq = C_PT + 2048
    C_raw = C_sq + 512
    C_V2 = C_raw + 512
    ATTN_C = C_V2 + NKT * 132
    aC = sb("aC", [128, max(FFN_C, ATTN_C)], BF16)

    xt_s = sb("xt", [128, 2 * D], F32)
    xo_s = sb("xo", [128, D], F32)
    mv_s = sb("mv", [128, 4], F32)
    bnst_s = sb("bnst", [128, 12], F32)
    rst_s = sb("rst", [128, 4], F32)
    nmr_s = sb("nmr", [128, 2], F32)
    epsc_s = sb("epsc", [128, 1], F32)
    gat_s = sb("gat", [128, 24], F32)
    w_s = sb("wcol", [128, 8], F32)
    llb_s = sb("llb", [128, 16], BF16)
    llT_s = sb("llT", [128, 128], BF16)
    wgA_s = sb("wgA_s", [128, DCH * 20], BF16)
    bgA_s = sb("bgA_s", [1, 20], BF16)
    ident_s = sb("ident_s", [128, 128], BF16)
    onesb_s = sb("onesb", [128, 128], BF16)
    wq_s = sb("wq_s", [128, DCH * 128], BF16)
    wk_s = sb("wk_s", [128, DCH * 128], BF16)
    wv_s = sb("wv_s", [128, DCH * 128], BF16)
    bq8_s = sb("bq8", [128, NPAIR], F32)
    bkp_s = sb("bkp_s", [128, NPAIR], F32)
    bvr_s = sb("bvr_s", [1, 128], BF16)
    boutr_s = sb("boutr_s", [1, D], BF16)
    sqm_s = sb("sqm", [1, 4 * NQB + 8], F32)
    negc_s = sb("negc", [128, 2], F32)
    dnl_s = sb("dnl", [1, 528], F32)
    dnr_s = sb("dnr", [1, 516], BF16)

    PALL = ps("pall", [128, 2048], F32)
    PA = [PALL[:, 0:1024], PALL[:, 1024:2048]]
    PC = [ps(f"pc{i}", [128, 512], F32) for i in range(3)]
    PT_ = ps("pbT", [128, 1024], BF16)

    for s in p.all_sems():
        s.h = ctx.enter_context(nc.semaphore(s.name))

    b_xt = [Buf("xt0"), Buf("xt1")]
    b_xo = [Buf("xo")]
    b_mv = [Buf("mv0"), Buf("mv1")]
    b_bnst = Buf("bnst")
    b_rst = [Buf("rst0"), Buf("rst1")]
    b_nmr = [Buf("nmr0"), Buf("nmr1")]
    b_xn = [Buf("xn0"), Buf("xn1")]
    b_xnT = [Buf("xnT0"), Buf("xnT1")]
    b_gat = Buf("gat")
    b_w = [Buf("w0"), Buf("w1")]
    b_llb = Buf("llb")
    b_llT = Buf("llT")
    b_basebf = [Buf("basebf0"), Buf("basebf1")]
    b_lorabf = Buf("lorabf")
    b_tmp = [Buf("tmp0"), Buf("tmp1")]
    b_hsum = [Buf(f"hsum{i}") for i in range(4)]
    b_hsumT = [Buf(f"hsumT{i}") for i in range(4)]
    b_xna = [Buf("xna0"), Buf("xna1")]
    b_xnat = [Buf(f"xnat{i}") for i in range(NT)]
    b_pa = [[Buf(f"pa{i}_{j}") for j in range(2)] for i in range(2)]
    b_pc = [Buf(f"pc{i}") for i in range(3)]
    b_pT = Buf("pbT")
    b_x1t = [Buf(f"x1t{i}") for i in range(NT)]
    b_x2t = [Buf(f"x2t{i}") for i in range(NT)]
    b_wiT = [[Buf(f"wiT{j}_{k}") for k in range(4)] for j in range(DCH)]
    b_woT = [Buf(f"woT{j}") for j in range(FF // 128)]
    b_wsm = Buf("w_small")
    b_consts = Buf("consts")
    b_wq = Buf("wq")
    b_bvp = Buf("bvp")
    b_wk = Buf("wk")
    b_wv = Buf("wv")
    b_xnTa = [Buf(f"xnTall{i}") for i in range(NT)]
    b_QT = [Buf("QT0"), Buf("QT1")]
    b_KT = [Buf("KT0"), Buf("KT1")]
    b_V = [Buf("V0"), Buf("V1")]
    b_PT = [Buf(f"PT{i}") for i in range(4)]
    b_sq = [Buf("sq0"), Buf("sq1")]
    b_sqm = Buf("sqm")
    b_negc = Buf("negc")
    b_dnl = Buf("dnl")
    b_rdnb = [Buf("rdnb0"), Buf("rdnb1")]
    b_AT = Buf("AT")
    b_woutT = [Buf(f"woutT{j}") for j in range(DCH)]

    def fwiT(j):
        return aA[:, j * FF:(j + 1) * FF]

    def fwoT(j):
        return aB[:, j * D:(j + 1) * D]

    def axnT(j):
        return aA[:, j * S:(j + 1) * S]

    def awoutT(j):
        return aA[:, j * D:(j + 1) * D]

    def aAT(pp):
        return aB[:, pp * S:(pp + 1) * S]

    p.emit("vector", [
        lambda e: e.memset(onesb_s[:], 1.0),
        lambda e: e.memset(epsc_s[:], 1e-5),
    ], writes=[b_consts])

    for dst, src in [(wgA_s[:].rearrange("p (c f) -> p c f", c=DCH),
                      wgA_d.rearrange("(c p) f -> p c f", p=128)),
                     (bgA_s[:], bgA_d),
                     (ident_s[:], ident_d),
                     (boutr_s[:], boutr_d)]:
        p.emit("gpsimd", [lambda e, dst=dst, src=src: e.dma_start(out=dst, in_=src)],
               writes=[b_wsm], dma=True)
    p.emit("sync", [lambda e: e.dma_start(out=bq8_s[:], in_=bqp_d)],
           writes=[b_wsm], dma=True)
    p.emit("sync", [lambda e: e.dma_start(out=bkp_s[:], in_=bkp_d)],
           writes=[b_wsm], dma=True)
    p.emit("vector", [lambda e: e.tensor_scalar(
        bq8_s[:], bq8_s[:], 0.125, None, op0=ALU.mult)], reads=[b_wsm], writes=[b_wsm])

    def load_ffn_weights():
        p.emit("gpsimd", [lambda e: e.dma_start(
            out=aC[:, C_bblk:C_bblk + FF], in_=bblk_d)], writes=[b_wsm], dma=True)
        for k in range(4):
            for j in range(DCH):
                p.emit("gpsimd", [lambda e, j=j, k=k: e.dma_start(
                    out=fwiT(j)[:, k * 1024:(k + 1) * 1024],
                    in_=wiT_d[j * 128:(j + 1) * 128, k * 1024:(k + 1) * 1024])],
                    writes=[b_wiT[j][k]], dma=True)
            for j in range(k * 8, k * 8 + 8):
                p.emit("gpsimd", [lambda e, j=j: e.dma_start(
                    out=fwoT(j), in_=woT_d[j * 128:(j + 1) * 128, :])],
                    writes=[b_woT[j]], dma=True)

    load_ffn_weights()

    def layernorm_tile(par, xt, xn_ap, b_xn_slot):
        mv = mv_s[:, par * 2:par * 2 + 2]
        lnv = rst_s[:, par * 2:par * 2 + 1]
        rstd = rst_s[:, par * 2 + 1:par * 2 + 2]
        nmr = nmr_s[:, par:par + 1]
        p.emit("vector", [
            lambda e: e.bn_stats(bnst_s[:, 0:6], xt[:, 0:512]),
            lambda e: e.bn_stats(bnst_s[:, 6:12], xt[:, 512:1024]),
        ], reads=[b_xt[par]], writes=[b_bnst])
        p.emit("vector", [lambda e: e.bn_aggr(mv, bnst_s[:])],
               reads=[b_bnst], writes=[b_mv[par]])
        p.emit("scalar", [lambda e: e.activation(lnv, mv[:, 1:2], AF.Ln, bias=epsc_s[:])],
               reads=[b_mv[par], b_consts], writes=[b_rst[par]])
        p.emit("scalar", [lambda e: e.activation(rstd, lnv, AF.Exp, scale=-0.5)],
               writes=[b_rst[par]])
        p.emit("vector", [lambda e: e.tensor_scalar(
            nmr, mv[:, 0:1], rstd, -1.0, op0=ALU.mult, op1=ALU.mult)],
            reads=[b_rst[par], b_mv[par]], writes=[b_nmr[par]])
        p.emit("scalar", [lambda e: e.activation(
            xn_ap, xt, AF.Identity, bias=nmr, scale=rstd)],
            reads=[b_xt[par], b_rst[par], b_nmr[par]], writes=[b_xn_slot])

    def transpose_xn(xn, dst_of_j, b_src, b_dst):
        p.emit("scalar", [
            (lambda e, j=j: e.dma_start(out=dst_of_j(j),
                                        in_=xn[:, j * 128:(j + 1) * 128],
                                        transpose=True))
            for j in range(0, DCH, 2)
        ], reads=[b_src], writes=[b_dst], dma=True)
        p.emit("sync", [
            (lambda e, j=j: e.dma_start(out=dst_of_j(j),
                                        in_=xn[:, j * 128:(j + 1) * 128],
                                        transpose=True))
            for j in range(1, DCH, 2)
        ], reads=[b_src], writes=[b_dst], dma=True)

    def layernorm_tile_on(par, xsrc, b_src, xn_ap, b_xn_slot):
        mv = mv_s[:, par * 2:par * 2 + 2]
        lnv = rst_s[:, par * 2:par * 2 + 1]
        rstd = rst_s[:, par * 2 + 1:par * 2 + 2]
        nmr = nmr_s[:, par:par + 1]
        p.emit("vector", [
            lambda e: e.bn_stats(bnst_s[:, 0:6], xsrc[:, 0:512]),
            lambda e: e.bn_stats(bnst_s[:, 6:12], xsrc[:, 512:1024]),
        ], reads=[b_src], writes=[b_bnst])
        p.emit("vector", [lambda e: e.bn_aggr(mv, bnst_s[:])],
               reads=[b_bnst], writes=[b_mv[par]])
        p.emit("scalar", [lambda e: e.activation(lnv, mv[:, 1:2], AF.Ln, bias=epsc_s[:])],
               reads=[b_mv[par], b_consts], writes=[b_rst[par]])
        p.emit("scalar", [lambda e: e.activation(rstd, lnv, AF.Exp, scale=-0.5)],
               writes=[b_rst[par]])
        p.emit("vector", [lambda e: e.tensor_scalar(
            nmr, mv[:, 0:1], rstd, -1.0, op0=ALU.mult, op1=ALU.mult)],
            reads=[b_rst[par], b_mv[par]], writes=[b_nmr[par]])
        p.emit("scalar", [lambda e: e.activation(
            xn_ap, xsrc, AF.Identity, bias=nmr, scale=rstd)],
            reads=[b_src, b_rst[par], b_nmr[par]], writes=[b_xn_slot])

    def ffn_phase(src_d, src_tiles, dst_d, dst_tiles, attn_prep=False):
        backlog = []

        def pop_backlog(n):
            for _ in range(min(n, len(backlog))):
                backlog.pop(0)()

        def load_tile(i):
            par = i % 2
            xt = xt_s[:, par * D:(par + 1) * D]
            if src_tiles is not None and src_tiles[i].ready is not None:
                p.raw_wait("sync", src_tiles[i].ready)
            p.emit("sync", [lambda e, i=i, xt=xt: e.dma_start(
                out=xt, in_=src_d[i * 128:(i + 1) * 128, :])],
                writes=[b_xt[par]], dma=True)

        def ln_tile(i):
            par = i % 2
            xt = xt_s[:, par * D:(par + 1) * D]
            xn = aC[:, C_xn[par]:C_xn[par] + 1024]
            xnT = aC[:, C_xnT[par]:C_xnT[par] + 1024]
            layernorm_tile(par, xt, xn, b_xn[par])
            transpose_xn(xn, lambda j, xnT=xnT: xnT[:, j * 128:(j + 1) * 128],
                         b_xn[par], b_xnT[par])

        for i in range(NT):
            par = i % 2
            xt = xt_s[:, par * D:(par + 1) * D]
            xo = xo_s[:, 0:D]
            xn = aC[:, C_xn[par]:C_xn[par] + 1024]
            xnT = aC[:, C_xnT[par]:C_xnT[par] + 1024]
            wc = w_s[:, par * 4:par * 4 + 4]

            if i == 0:
                load_tile(0)
                ln_tile(0)
                if NT > 1:
                    load_tile(1)
            if i + 1 < NT:
                ln_tile(i + 1)

            def base_mm(fc):
                p.emit("tensor", [
                    (lambda e, j=j, fc=fc, xnT=xnT: e.matmul(
                        PC[0][:, :], xnT[:, j * 128:(j + 1) * 128],
                        fwiT(j)[:, fc * 512:(fc + 1) * 512],
                        start=(j == 0), stop=(j == DCH - 1)))
                    for j in range(DCH)
                ], reads=[b_xnT[par]] + [b_wiT[j][fc // 2] for j in range(DCH)],
                    writes=[b_pc[0]])
                basebf = aC[:, C_base[fc % 2]:C_base[fc % 2] + 512]
                p.emit("scalar", [lambda e, basebf=basebf: e.copy(basebf, PC[0][:, :])],
                       reads=[b_pc[0]], writes=[b_basebf[fc % 2]])

            pg = PA[0][:, 0:20]
            mm = [(lambda e, j=j, xnT=xnT: e.matmul(
                pg, xnT[:, j * 128:(j + 1) * 128], wgA_s[:, j * 20:(j + 1) * 20],
                start=(j == 0), stop=False)) for j in range(DCH)]
            mm.append(lambda e: e.matmul(pg, onesb_s[0:1, :], bgA_s[:],
                                         start=False, stop=True))
            p.emit("tensor", mm, reads=[b_xnT[par], b_wsm, b_consts],
                   writes=[b_pa[0][0]])

            g = gat_s
            p.emit("vector", [lambda e: e.tensor_reduce(
                g[:, 4:5], PA[0][:, 0:4], AX.X, ALU.max, negate=True)],
                reads=[b_pa[0][0]], writes=[b_gat])
            p.emit("scalar", [lambda e: e.activation(
                g[:, 0:4], PA[0][:, 0:4], AF.Exp, bias=g[:, 4:5],
                accum_out=g[:, 5:6])], reads=[b_pa[0][0], b_gat], writes=[b_gat])
            p.emit("vector", [lambda e: e.reciprocal(g[:, 4:5], g[:, 5:6])],
                   writes=[b_gat])
            p.emit("vector", [
                lambda e: e.tensor_scalar(g[:, 4:5], g[:, 4:5], 0.5, None, op0=ALU.mult),
            ], writes=[b_gat])
            p.emit("vector", [
                lambda e: e.tensor_scalar(g[:, 6:10], g[:, 0:4], g[:, 4:5], None,
                                          op0=ALU.mult),
            ], writes=[b_gat])
            rank_fns = []
            for ee in range(E):
                if ee > 0:
                    rank_fns.append(lambda e, ee=ee: e.tensor_scalar(
                        g[:, 18:18 + ee], g[:, 6:6 + ee], g[:, 6 + ee:7 + ee], None,
                        op0=ALU.is_ge))
                    rank_fns.append(lambda e, ee=ee: e.tensor_reduce(
                        g[:, 10 + ee:11 + ee], g[:, 18:18 + ee], AX.X, ALU.add))
                if ee < E - 1:
                    rank_fns.append(lambda e, ee=ee: e.tensor_scalar(
                        g[:, 18:18 + (E - 1 - ee)], g[:, 7 + ee:10],
                        g[:, 6 + ee:7 + ee], None, op0=ALU.is_gt))
                    rank_fns.append(lambda e, ee=ee: e.tensor_reduce(
                        g[:, 22:23], g[:, 18:18 + (E - 1 - ee)], AX.X, ALU.add))
                    if ee > 0:
                        rank_fns.append(lambda e, ee=ee: e.tensor_tensor(
                            g[:, 10 + ee:11 + ee], g[:, 10 + ee:11 + ee],
                            g[:, 22:23], ALU.add))
                    else:
                        rank_fns.append(lambda e: e.tensor_copy(g[:, 10:11], g[:, 22:23]))
            rank_fns.append(lambda e: e.tensor_scalar(
                g[:, 14:18], g[:, 10:14], 1.5, None, op0=ALU.is_lt))
            rank_fns.append(lambda e, wc=wc: e.tensor_tensor(
                wc, g[:, 14:18], g[:, 6:10], ALU.mult))
            for fn in rank_fns:
                p.emit("vector", [fn], writes=[b_gat, b_w[par]])

            p.emit("scalar", [lambda e: e.copy(llb_s[:], PA[0][:, 4:20])],
                   reads=[b_pa[0][0]], writes=[b_llb])
            p.emit("tensor", [lambda e: e.transpose(
                PT_[0:16, 512:640], llb_s[:], ident_s[:])],
                reads=[b_llb, b_wsm], writes=[b_pT])
            p.emit("scalar", [
                (lambda e, ee=ee: e.copy(llT_s[32 * ee:32 * ee + 16, :],
                                         PT_[0:16, 512:640]))
                for ee in range(E)
            ], reads=[b_pT], writes=[b_llT])

            for fc in range(NFC):
                fpar = fc % 2
                f4 = fc % 4
                basebf = aC[:, C_base[fpar]:C_base[fpar] + 512]
                hsum = aC[:, C_hsum[f4]:C_hsum[f4] + 512]
                hsumT = aC[:, C_hsumT[f4]:C_hsumT[f4] + 512]
                base_mm(fc)
                for ee in range(E):
                    p.emit("tensor", [lambda e, ee=ee, fc=fc: e.matmul(
                        PA[ee // 2][:, (ee % 2) * 512:(ee % 2) * 512 + 512],
                        llT_s[32 * ee:32 * ee + 16, :],
                        aC[32 * ee:32 * ee + 16, C_bblk + fc * 512:C_bblk + (fc + 1) * 512],
                        start=True, stop=True, tile_position=(32 * ee, 0))],
                        reads=[b_llT, b_wsm], writes=[b_pa[ee // 2][ee % 2]])
                pop_backlog(3 if fc == 0 else 2)
                if fc == 2 and i + 1 < NT:
                    ln_tile(i + 1)
                tl = 12288 + fpar * 2048
                tlb = aC[:, tl:tl + 2048]
                bl = b_tmp[fpar]
                p.emit("vector", [
                    lambda e, tlb=tlb, basebf=basebf: e.tensor_tensor(
                        tlb.rearrange("p (g f) -> p g f", g=E),
                        PALL[:, :].rearrange("p (g f) -> p g f", g=E),
                        basebf.rearrange("p (g f) -> p g f", g=1).broadcast_to(
                            [128, E, 512]),
                        ALU.add)
                ], reads=[b_basebf[fpar], b_pa[0][0], b_pa[0][1], b_pa[1][0],
                          b_pa[1][1]], writes=[bl])
                p.emit("vector", [
                    (lambda e, ee=ee, tlb=tlb, wc=wc: e.tensor_scalar(
                        tlb[:, ee * 512:(ee + 1) * 512],
                        tlb[:, ee * 512:(ee + 1) * 512],
                        wc[:, ee:ee + 1], 0.0, op0=ALU.mult, op1=ALU.max))
                    for ee in range(E)
                ], reads=[b_w[par]], writes=[bl])
                p.emit("vector", [
                    lambda e, tlb=tlb: e.tensor_tensor(
                        tlb[:, 0:512], tlb[:, 0:512], tlb[:, 512:1024], ALU.add),
                    lambda e, tlb=tlb: e.tensor_tensor(
                        tlb[:, 1024:1536], tlb[:, 1024:1536], tlb[:, 1536:2048],
                        ALU.add),
                ], writes=[bl])
                p.emit("vector", [lambda e, tlb=tlb, hsum=hsum: e.tensor_tensor(
                    hsum, tlb[:, 0:512], tlb[:, 1024:1536], ALU.add)],
                    reads=[bl], writes=[b_hsum[f4]])

                def defer_tail(fc=fc, f4=f4, hsum=hsum, hsumT=hsumT):
                    def t1():
                        p.emit("tensor", [
                            (lambda e, q=q: e.transpose(
                                PT_[:, (f4 % 2) * 512 + q * 128:
                                    (f4 % 2) * 512 + (q + 1) * 128],
                                hsum[:, q * 128:(q + 1) * 128], ident_s[:]))
                            for q in range(4)
                        ], reads=[b_hsum[f4], b_wsm], writes=[b_pT])
                        p.emit("scalar", [lambda e: e.copy(
                            hsumT, PT_[:, (f4 % 2) * 512:(f4 % 2) * 512 + 512])],
                            reads=[b_pT], writes=[b_hsumT[f4]])

                    def t2():
                        p.emit("tensor", [
                            (lambda e, q=q, dm=dm: e.matmul(
                                PC[1 + dm][:, :],
                                hsumT[:, q * 128:(q + 1) * 128],
                                fwoT(fc * 4 + q)[:, dm * 512:(dm + 1) * 512],
                                start=(fc == 0 and q == 0),
                                stop=(fc == NFC - 1 and q == 3)))
                            for dm in range(2) for q in range(4)
                        ], reads=[b_hsumT[f4]] + b_woT,
                            writes=[b_pc[1], b_pc[2]])
                    return [t1, t2]

                backlog.extend(defer_tail())

            def epilogue(i=i, par=par, xo=xo, xt=xt):
                p.emit("vector", [
                    lambda e: e.tensor_tensor(
                        xo[:, 0:512], PC[1][:, :], xt[:, 0:512], ALU.add),
                    lambda e: e.tensor_tensor(
                        xo[:, 512:1024], PC[2][:, :], xt[:, 512:1024], ALU.add),
                ], reads=[b_pc[1], b_pc[2], b_xt[par]], writes=[b_xo[0]])
                p.emit("gpsimd", [lambda e: e.dma_start(
                    out=dst_d[i * 128:(i + 1) * 128, :], in_=xo)],
                    reads=[b_xo[0]],
                    writes=([dst_tiles[i]] if dst_tiles is not None else []),
                    dma=True)
                if attn_prep:
                    xna = aC[:, C_xnatt[par]:C_xnatt[par] + 1024]
                    layernorm_tile_on(par, xo, b_xo[0], xna, b_xna[par])
                    p.emit("gpsimd", [lambda e: e.dma_start(
                        out=xna_d[i * 128:(i + 1) * 128, :], in_=xna)],
                        reads=[b_xna[par]], writes=[b_xnat[i]], dma=True)
                if i + 2 < NT:
                    load_tile(i + 2)

            backlog.append(epilogue)
        pop_backlog(len(backlog))

    ffn_phase(x_in, None, x1_d, b_x1t, attn_prep=(n_phases > 1))
    if n_phases == 1:
        finish(nc, p, ctx)
        return nc
    p.barrier()

    for i in range(NT):
        p.raw_wait("scalar", b_xnat[i].ready)
        p.raw_wait("sync", b_xnat[i].ready)
        p.emit("scalar", [
            (lambda e, j=j, i=i: e.dma_start(
                out=axnT(j)[:, i * 128:(i + 1) * 128],
                in_=xna_d[i * 128:(i + 1) * 128, j * 128:(j + 1) * 128],
                transpose=True))
            for j in range(0, DCH, 2)
        ], writes=[b_xnTa[i]], dma=True)
        p.emit("sync", [
            (lambda e, j=j, i=i: e.dma_start(
                out=axnT(j)[:, i * 128:(i + 1) * 128],
                in_=xna_d[i * 128:(i + 1) * 128, j * 128:(j + 1) * 128],
                transpose=True))
            for j in range(1, DCH, 2)
        ], writes=[b_xnTa[i]], dma=True)

    if ATTN_STAGE == 1:
        finish(nc, p, ctx)
        return nc

    p.emit("vector", [lambda e: e.memset(aC[:, C_V:C_V + NKT * 132], 1.0)],
           writes=[b_V[0]])
    p.emit("vector", [lambda e: e.memset(aC[:, C_V2:C_V2 + NKT * 132], 1.0)],
           writes=[b_V[1]])

    def vbase(pp):
        return C_V if pp % 2 == 0 else C_V2

    def QTb(par):
        return aC[:, C_QT + par * S:C_QT + (par + 1) * S]

    def KTb(par):
        return aC[:, C_KT + par * S:C_KT + (par + 1) * S]

    def load_pair_w(pp):
        p.emit("gpsimd", [lambda e, pp=pp: e.dma_start(
            out=bvr_s[:], in_=bvr_d[0:1, pp * 128:(pp + 1) * 128])],
            writes=[b_bvp], dma=True)
        for wdst, wsrc, bb in [(wq_s, wqp_d, b_wq), (wk_s, wkp_d, b_wk),
                               (wv_s, wvp_d, b_wv)]:
            p.emit("gpsimd", [lambda e, wdst=wdst, wsrc=wsrc, pp=pp: e.dma_start(
                out=wdst[:].rearrange("p (c m) -> p c m", c=DCH),
                in_=wsrc[pp * D:(pp + 1) * D, :].rearrange("(c p) m -> p c m", p=128))],
                writes=[bb], dma=True)

    def qkv_group(pp, kind, qb, bank_ap, bank_buf):
        par = pp % 2
        wsrc = wq_s if kind == 0 else wk_s
        dst = QTb(par) if kind == 0 else KTb(par)
        bdst = b_QT[par] if kind == 0 else b_KT[par]
        bwsrc = b_wq if kind == 0 else b_wk
        bcol = bq8_s[:, pp:pp + 1] if kind == 0 else bkp_s[:, pp:pp + 1]
        scl = 0.125 if kind == 0 else 1.0
        tiles = [b_xnTa[t] for t in range(qb * 4, qb * 4 + 4)]
        p.emit("tensor", [
            (lambda e, j=j, qb=qb, wsrc=wsrc, bank_ap=bank_ap: e.matmul(
                bank_ap, wsrc[:, j * 128:(j + 1) * 128],
                axnT(j)[:, qb * 512:(qb + 1) * 512],
                start=(j == 0), stop=(j == DCH - 1)))
            for j in range(DCH)
        ], reads=[bwsrc] + tiles, writes=[bank_buf])
        p.emit("vector", [lambda e, qb=qb, dst=dst, bank_ap=bank_ap, bcol=bcol,
               scl=scl: e.tensor_scalar(
                   dst[:, qb * 512:(qb + 1) * 512], bank_ap,
                   scl, bcol, op0=ALU.mult, op1=ALU.add)],
               reads=[bank_buf, b_wsm], writes=[bdst])

    def v_group(pp, kt):
        mm = [(lambda e, j=j, kt=kt: e.matmul(
            PC[2][:, 0:128], axnT(j)[:, kt * 128:(kt + 1) * 128],
            wv_s[:, j * 128:(j + 1) * 128],
            start=(j == 0), stop=False)) for j in range(DCH)]
        mm.append(lambda e: e.matmul(
            PC[2][:, 0:128], onesb_s[0:1, :], bvr_s[0:1, :],
            start=False, stop=True))
        p.emit("tensor", mm, reads=[b_wv, b_xnTa[kt], b_consts, b_bvp],
               writes=[b_pc[2]])
        vb = vbase(pp) + kt * 132
        p.emit("vector", [
            lambda e, vb=vb: e.tensor_copy(aC[:, vb:vb + 64], PC[2][:, 0:64]),
            lambda e, vb=vb: e.tensor_copy(aC[:, vb + 66:vb + 130],
                                           PC[2][:, 64:128]),
        ], reads=[b_pc[2]], writes=[b_V[pp % 2]])

    def v_phase(pp):
        for kt in range(NKT):
            v_group(pp, kt)

    def stab_phase(pp):
        par = pp % 2
        QT, KT = QTb(par), KTb(par)
        for kind in range(2):
            src2 = QT if kind == 0 else KT
            bsrc = b_QT[par] if kind == 0 else b_KT[par]
            for qb in range(NQB):
                spar = 0
                sq = aC[:, C_sq:C_sq + 512]
                p.emit("scalar", [lambda e, src2=src2, sq=sq, qb=qb: e.activation(
                    sq, src2[:, qb * 512:(qb + 1) * 512], AF.Square)],
                    reads=[bsrc], writes=[b_sq[spar]])
                for half in range(2):
                    npar = (qb * 2 + half) % 2
                    nbank = PC[2][0:1, 0:512] if npar == 0 else PA[1][0:1, 0:512]
                    nbuf = b_pc[2] if npar == 0 else b_pa[1][0]
                    p.emit("tensor", [lambda e, half=half, sq=sq, nbank=nbank:
                           e.matmul(
                        nbank, onesb_s[half * 64:half * 64 + 64, 0:1],
                        sq[half * 64:half * 64 + 64, :],
                        start=True, stop=True, tile_position=(half * 64, 0))],
                        reads=[b_sq[spar], b_consts], writes=[nbuf])
                    off = 0 if half == 0 else 2 * NQB + 4
                    p.emit("vector", [lambda e, off=off, kind=kind, qb=qb,
                           nbank=nbank: e.tensor_reduce(
                               sqm_s[0:1, off + kind * NQB + qb:
                                     off + kind * NQB + qb + 1],
                               nbank, AX.X, ALU.max)],
                           reads=[nbuf], writes=[b_sqm])
        for half in range(2):
            off = 0 if half == 0 else 2 * NQB + 4
            for kind in range(2):
                p.emit("vector", [lambda e, off=off, kind=kind, half=half:
                       e.tensor_reduce(
                           sqm_s[0:1, 2 * NQB + half * 2 + kind:
                                 2 * NQB + half * 2 + kind + 1],
                           sqm_s[0:1, off + kind * NQB:off + (kind + 1) * NQB],
                           AX.X, ALU.max)],
                       reads=[b_sqm], writes=[b_sqm])
            p.emit("vector", [lambda e, half=half: e.tensor_tensor(
                dnl_s[0:1, half:half + 1],
                sqm_s[0:1, 2 * NQB + half * 2:2 * NQB + half * 2 + 1],
                sqm_s[0:1, 2 * NQB + half * 2 + 1:2 * NQB + half * 2 + 2],
                ALU.mult)], reads=[b_sqm], writes=[b_dnl])
        p.emit("scalar", [lambda e: e.activation(dnl_s[0:1, 2:4], dnl_s[0:1, 0:2],
                                                 AF.Ln)],
               reads=[b_dnl], writes=[b_dnl])
        p.emit("scalar", [lambda e: e.activation(dnl_s[0:1, 4:6], dnl_s[0:1, 2:4],
                                                 AF.Exp, scale=0.5)],
               writes=[b_dnl])
        p.emit("vector", [lambda e: e.tensor_scalar(
            dnr_s[0:1, 512:514], dnl_s[0:1, 4:6], -1.0, None, op0=ALU.mult)],
            reads=[b_dnl], writes=[b_dnl])
        p.emit("tensor", [lambda e: e.matmul(
            PC[2][0:128, 0:2], onesb_s[0:1, :], dnr_s[0:1, 512:514],
            start=True, stop=True)],
            reads=[b_dnl, b_consts], writes=[b_pc[2]])
        p.emit("scalar", [lambda e: e.copy(negc_s[:], PC[2][:, 0:2])],
               reads=[b_pc[2]], writes=[b_negc])

    load_pair_w(0)
    for kind in range(2):
        for qb in range(NQB):
            qkv_group(0, kind, qb, PC[qb % 2][:, :], b_pc[qb % 2])
    v_phase(0)
    stab_phase(0)

    for pp in range(NPAIR):
        par = pp % 2
        QT, KT = QTb(par), KTb(par)
        if pp + 1 < NPAIR:
            load_pair_w(pp + 1)
        qkv_backlog = []
        if pp + 1 < NPAIR:
            for kind in range(2):
                for qb in range(NQB):
                    qkv_backlog.append(
                        lambda kind=kind, qb=qb: qkv_group(
                            pp + 1, kind, qb, PC[2][:, :], b_pc[2]))
            for kt in range(NKT):
                qkv_backlog.append(lambda kt=kt: v_group(pp + 1, kt))

        if ATTN_STAGE == 2:
            continue
        if ATTN_STAGE == 3:
            continue
        for qb in range(NQB):
            abl = []
            for kt in range(NKT):
                sub = kt % 2
                for half in range(2):
                    p.emit("tensor", [lambda e, half=half, kt=kt, qb=qb, sub=sub,
                           QT=QT, KT=KT: e.matmul(
                               PA[half][:, sub * 512:sub * 512 + 512],
                               KT[half * 64:half * 64 + 64,
                                  kt * 128:(kt + 1) * 128],
                               QT[half * 64:half * 64 + 64,
                                  qb * 512:(qb + 1) * 512],
                               start=True, stop=True,
                               tile_position=(half * 64, 0))],
                           reads=[b_QT[par], b_KT[par]],
                           writes=[b_pa[half][sub]])
                for half in range(2):
                    ptb = C_PT + (half * 2 + sub) * 512
                    p.emit("scalar", [lambda e, half=half, ptb=ptb, sub=sub:
                           e.activation(
                        aC[:, ptb:ptb + 512], PA[half][:, sub * 512:sub * 512 + 512],
                        AF.Exp, bias=negc_s[:, half:half + 1])],
                        reads=[b_pa[half][sub], b_negc],
                        writes=[b_PT[half * 2 + sub]])

                    def av(kt=kt, half=half, ptb=ptb, sub=sub, qb=qb):
                        ab = PC[half]
                        bab = b_pc[half]
                        vb = vbase(pp) + kt * 132 + half * 66
                        p.emit("tensor", [lambda e, ab=ab, vb=vb, ptb=ptb, kt=kt:
                               e.matmul(
                                   ab[0:65, :], aC[:, vb:vb + 65],
                                   aC[:, ptb:ptb + 512],
                                   start=(kt == 0), stop=(kt == NKT - 1))],
                               reads=[b_PT[half * 2 + sub], b_V[pp % 2]],
                               writes=[bab])
                    abl.append(av)
                if kt >= 1:
                    abl.pop(0)()
                    abl.pop(0)()
                if qkv_backlog and kt % 2 == 1:
                    qkv_backlog.pop(0)()
            while abl:
                abl.pop(0)()

            for half in range(2):
                ab = PC[half]
                bab = b_pc[half]
                rpar = (qb * 2 + half) % 2
                rawb = aC[rpar * 64:rpar * 64 + 64, C_raw:C_raw + 512]
                p.emit("scalar", [lambda e, ab=ab: e.activation(
                    dnl_s[0:1, 16:528], ab[64:65, :], AF.Ln)],
                    reads=[bab], writes=[b_dnl])
                p.emit("vector", [lambda e, ab=ab, rawb=rawb: e.tensor_copy(
                    rawb, ab[0:64, :])],
                    reads=[bab], writes=[b_rdnb[rpar]])
                p.emit("scalar", [lambda e: e.activation(
                    dnr_s[0:1, 0:512], dnl_s[0:1, 16:528], AF.Exp, scale=-1.0)],
                    writes=[b_dnl])
                p.emit("tensor", [lambda e, ab=ab: e.matmul(
                    ab[0:128, :], onesb_s[0:1, :], dnr_s[0:1, 0:512],
                    start=True, stop=True)],
                    reads=[b_dnl, b_consts], writes=[bab])
                p.emit("vector", [lambda e, ab=ab, rawb=rawb, pp=pp, half=half,
                       qb=qb: e.tensor_tensor(
                           aAT(pp)[half * 64:half * 64 + 64,
                                   qb * 512:(qb + 1) * 512],
                           rawb, ab[0:64, :], ALU.mult)],
                       reads=[bab], writes=[b_AT])
        while qkv_backlog:
            qkv_backlog.pop(0)()
        if pp + 1 < NPAIR:
            stab_phase(pp + 1)

    for j in range(DCH):
        p.emit("gpsimd", [lambda e, j=j: e.dma_start(
            out=awoutT(j), in_=woutT_d[j * 128:(j + 1) * 128, :])],
            writes=[b_woutT[j]] + b_xnTa, dma=True)
    for i in range(NT):
        par = i % 2
        xt = xt_s[:, par * D:(par + 1) * D]
        xo = xo_s[:, 0:D]
        if par == 0:
            banks = [(PC[0][:, :], b_pc[0]), (PC[1][:, :], b_pc[1])]
        else:
            banks = [(PA[0][:, 0:512], b_pa[0][0]), (PA[0][:, 512:1024], b_pa[0][1])]
        p.emit("sync", [lambda e, i=i, xt=xt: e.dma_start(
            out=xt, in_=x1_d[i * 128:(i + 1) * 128, :])],
            writes=[b_xt[par]], dma=True)
        for dm in range(2):
            pbk, bbk = banks[dm]
            mm = [(lambda e, ppx=ppx, i=i, dm=dm, pbk=pbk: e.matmul(
                pbk, aAT(ppx)[:, i * 128:(i + 1) * 128],
                awoutT(ppx)[:, dm * 512:(dm + 1) * 512],
                start=(ppx == 0), stop=False)) for ppx in range(NPAIR)]
            mm.append(lambda e, dm=dm, pbk=pbk: e.matmul(
                pbk, onesb_s[0:1, :], boutr_s[:, dm * 512:(dm + 1) * 512],
                start=False, stop=True))
            p.emit("tensor", mm,
                   reads=[b_AT, b_consts, b_wsm] + b_woutT, writes=[bbk])
        p.emit("vector", [
            lambda e, xo=xo, xt=xt, banks=banks: e.tensor_tensor(
                xo[:, 0:512], banks[0][0], xt[:, 0:512], ALU.add),
            lambda e, xo=xo, xt=xt, banks=banks: e.tensor_tensor(
                xo[:, 512:1024], banks[1][0], xt[:, 512:1024], ALU.add),
        ], reads=[banks[0][1], banks[1][1], b_xt[par]], writes=[b_xo[0]])
        p.emit("sync", [lambda e, i=i, xo=xo: e.dma_start(
            out=x2_d[i * 128:(i + 1) * 128, :], in_=xo)],
            reads=[b_xo[0]], writes=[b_x2t[i]], dma=True)

    if n_phases == 2:
        finish(nc, p, ctx)
        return nc

    def toks_of(bufs):
        out = []
        for b in bufs:
            if b.ready is not None:
                out.append(b.ready)
            out.extend(b.readers)
        return out

    attn_toks = toks_of(
        b_QT + b_KT + b_V + [b_sq[0], b_AT, b_negc, b_sqm, b_dnl] + b_PT + b_rdnb)
    at_toks = toks_of([b_AT] + b_woutT)
    xnta_toks = toks_of(b_xnTa)
    for b in (b_xn + b_xnT + b_hsum + b_hsumT + b_basebf + b_tmp + b_xna + [b_wsm]):
        b.readers.extend(attn_toks)
    for j in range(DCH):
        for k in range(4):
            b_wiT[j][k].readers.extend(xnta_toks)
            if j < 2:
                b_wiT[j][k].readers.extend(at_toks)
    for j in range(FF // 128):
        b_woT[j].readers.extend(at_toks)

    load_ffn_weights()
    ffn_phase(x2_d, b_x2t, y_out, None)

    finish(nc, p, ctx)
    return nc


def finish(nc, p, ctx):
    for q in ("sync", "scalar", "gpsimd"):
        for s in p.dma_pools[q]:
            if s.n > 0:
                p.raw_wait("sync", (s, s.n))
    with nc.Block() as block:
        @block.sync
        def _(e):
            for f in p.ops["sync"]:
                f(e)

        @block.tensor
        def _(e):
            for f in p.ops["tensor"]:
                f(e)

        @block.vector
        def _(e):
            for f in p.ops["vector"]:
                f(e)

        @block.scalar
        def _(e):
            for f in p.ops["scalar"]:
                f(e)

        @block.gpsimd
        def _(e):
            for f in p.ops["gpsimd"]:
                f(e)
    ctx.close()


def kernel(**inputs):
    S = 4096
    x = np.asarray(inputs["L0_attn_x"], np.float32)
    lay = host_layout(inputs)
    nc = build_nc(S)
    in_maps = []
    for c in range(NCORES):
        m = {"x": np.ascontiguousarray(x[c])}
        m.update(lay)
        in_maps.append(m)
    res = run_bass_kernel_spmd(nc, in_maps, core_ids=list(range(NCORES)))
    out = np.stack([res.results[c]["y"] for c in range(NCORES)], axis=0)
    return out.astype(np.float32)


# version 32
# speedup vs baseline: 1.0300x; 1.0025x over previous
import numpy as np
import ml_dtypes
from contextlib import ExitStack

import concourse.bass as bass
import concourse.mybir as mybir
from concourse.bass_utils import run_bass_kernel_spmd

F32 = mybir.dt.float32
BF16 = mybir.dt.bfloat16
AF = mybir.ActivationFunctionType
ALU = mybir.AluOpType
AX = mybir.AxisListType

D = 1024
FF = 4096
E = 4
RK = 4
DH = 64
NPAIR = 8
NCORES = 8
DCH = D // 128

ENGINES = ("sync", "tensor", "vector", "scalar", "gpsimd")
ATTN_STAGE = 0
LOOP_PART = 4


class Sem:
    def __init__(self, name, step):
        self.name, self.step, self.n, self.h = name, step, 0, None


class Buf:
    __slots__ = ("name", "ready", "readers")

    def __init__(self, name):
        self.name, self.ready, self.readers = name, None, []


NDMASEM = 8


class Prog:
    def __init__(self):
        self.ops = {e: [] for e in ENGINES}
        self.sems = {e: Sem("c_" + e, 1) for e in ENGINES}
        self.dma_pools = {q: [Sem(f"d_{q}{i}", 16) for i in range(NDMASEM)]
                          for q in ("sync", "scalar", "gpsimd")}
        self.dma_idx = {q: 0 for q in self.dma_pools}
        self.waited = {e: {} for e in ENGINES}

    def all_sems(self):
        out = list(self.sems.values())
        for pool in self.dma_pools.values():
            out.extend(pool)
        return out

    def _wait(self, eng, tok):
        if tok is None:
            return
        if isinstance(tok, list):
            for t in tok:
                self._wait(eng, t)
            return
        sem, val = tok
        if self.waited[eng].get(sem.name, 0) >= val:
            return
        self.waited[eng][sem.name] = val
        self.ops[eng].append(lambda ee, s=sem, v=val: ee.wait_ge(s.h, v))

    def emit(self, eng, fns, reads=(), writes=(), dma=False):
        for b in reads:
            self._wait(eng, b.ready)
        for b in writes:
            self._wait(eng, b.ready)
            for t in b.readers:
                self._wait(eng, t)
        if dma:
            toks = []
            for fn in fns:
                pool = self.dma_pools[eng]
                sem = pool[self.dma_idx[eng]]
                self.dma_idx[eng] = (self.dma_idx[eng] + 1) % NDMASEM
                if sem.n > 0:
                    self._wait(eng, (sem, sem.n))
                sem.n += 16
                self.ops[eng].append(
                    lambda ee, f=fn, s=sem: f(ee).then_inc(s.h, 16))
                toks.append((sem, sem.n))
            tok = toks if len(toks) > 1 else toks[0]
        else:
            sem = self.sems[eng]
            n = len(fns)
            for i, fn in enumerate(fns):
                if i == n - 1:
                    sem.n += 1
                    self.ops[eng].append(
                        lambda ee, f=fn, s=sem: f(ee).then_inc(s.h, 1))
                else:
                    self.ops[eng].append(lambda ee, f=fn: f(ee))
            tok = (sem, sem.n)
        for b in reads:
            b.readers.append(tok)
        for b in writes:
            b.ready = tok
            b.readers = []
        return tok

    def raw_wait(self, eng, tok):
        self._wait(eng, tok)

    def barrier(self):
        toks = [(s, s.n) for s in self.all_sems() if s.n > 0]
        for e in ENGINES:
            own = self.sems[e]
            for t in toks:
                if t[0] is not own:
                    self._wait(e, t)


def host_layout(inputs):
    W_qkv = np.asarray(inputs["L0_attn_W_qkv"], np.float32)
    b_qkv = np.asarray(inputs["L0_attn_b_qkv"], np.float32)
    W_out = np.asarray(inputs["L0_attn_W_out"], np.float32)
    b_out = np.asarray(inputs["L0_attn_b_out"], np.float32)
    Wg = np.asarray(inputs["L0_ffn_Wg"], np.float32)
    bg = np.asarray(inputs["L0_ffn_bg"], np.float32)
    Wi = np.asarray(inputs["L0_ffn_Wi"], np.float32)
    Wo = np.asarray(inputs["L0_ffn_Wo"], np.float32)
    A = np.asarray(inputs["L0_ffn_A"], np.float32)
    B = np.asarray(inputs["L0_ffn_B"], np.float32)

    out = {}
    out["wiT"] = np.ascontiguousarray(Wi.T)
    out["woT"] = np.ascontiguousarray(Wo.T)
    out["wgA"] = np.ascontiguousarray(
        np.concatenate([Wg.T, A.reshape(E * RK, D).T], axis=1))
    out["bgA"] = np.ascontiguousarray(
        np.concatenate([bg, np.zeros(E * RK, np.float32)])[None, :])
    bblk = np.zeros((128, FF), np.float32)
    for e in range(E):
        for r in range(RK):
            bblk[32 * e + 4 * e + r, :] = B[e, :, r]
    out["bblk"] = bblk
    wq = np.zeros((NPAIR * D, 128), np.float32)
    wk = np.zeros((NPAIR * D, 128), np.float32)
    wv = np.zeros((NPAIR * D, 128), np.float32)
    bq = np.zeros((128, NPAIR), np.float32)
    bk = np.zeros((128, NPAIR), np.float32)
    bv = np.zeros((1, D), np.float32)
    for p_ in range(NPAIR):
        for j in range(2):
            h = 2 * p_ + j
            base = h * 3 * DH
            wq[p_ * D:(p_ + 1) * D, j * DH:(j + 1) * DH] = W_qkv[base:base + DH, :].T
            wk[p_ * D:(p_ + 1) * D, j * DH:(j + 1) * DH] = W_qkv[base + DH:base + 2 * DH, :].T
            wv[p_ * D:(p_ + 1) * D, j * DH:(j + 1) * DH] = W_qkv[base + 2 * DH:base + 3 * DH, :].T
            bq[j * DH:(j + 1) * DH, p_] = b_qkv[base:base + DH]
            bk[j * DH:(j + 1) * DH, p_] = b_qkv[base + DH:base + 2 * DH]
            bv[0, p_ * 128 + j * DH:p_ * 128 + (j + 1) * DH] = \
                b_qkv[base + 2 * DH:base + 3 * DH]
    out["wqp"], out["wkp"], out["wvp"] = wq, wk, wv
    out["bqp"], out["bkp"], out["bvr"] = bq, bk, bv
    out["woutT"] = np.ascontiguousarray(W_out.T)
    out["boutr"] = np.ascontiguousarray(b_out[None, :])
    out["ident"] = np.eye(128, dtype=np.float32)
    return out


def build_nc(S, n_phases=3):
    NT = S // 128
    NFC = FF // 512
    NQB = S // 512
    NKT = S // 128

    nc = bass.Bass("TRN2", target_bir_lowering=False)
    p = Prog()

    x_in = nc.dram_tensor("x", [S, D], F32, kind="ExternalInput").ap()
    y_out = nc.dram_tensor("y", [S, D], F32, kind="ExternalOutput").ap()
    x1_d = nc.dram_tensor("x1_scratch", [S, D], F32, kind="Internal").ap()
    x2_d = nc.dram_tensor("x2_scratch", [S, D], F32, kind="Internal").ap()
    xna_d = nc.dram_tensor("xna_scratch", [S, D], BF16, kind="Internal").ap()
    wiT_d = nc.dram_tensor("wiT", [D, FF], F32, kind="ExternalInput").ap()
    woT_d = nc.dram_tensor("woT", [FF, D], F32, kind="ExternalInput").ap()
    wgA_d = nc.dram_tensor("wgA", [D, 20], F32, kind="ExternalInput").ap()
    bgA_d = nc.dram_tensor("bgA", [1, 20], F32, kind="ExternalInput").ap()
    bblk_d = nc.dram_tensor("bblk", [128, FF], F32, kind="ExternalInput").ap()
    wqp_d = nc.dram_tensor("wqp", [NPAIR * D, 128], F32, kind="ExternalInput").ap()
    wkp_d = nc.dram_tensor("wkp", [NPAIR * D, 128], F32, kind="ExternalInput").ap()
    wvp_d = nc.dram_tensor("wvp", [NPAIR * D, 128], F32, kind="ExternalInput").ap()
    bqp_d = nc.dram_tensor("bqp", [128, NPAIR], F32, kind="ExternalInput").ap()
    bkp_d = nc.dram_tensor("bkp", [128, NPAIR], F32, kind="ExternalInput").ap()
    bvr_d = nc.dram_tensor("bvr", [1, D], F32, kind="ExternalInput").ap()
    woutT_d = nc.dram_tensor("woutT", [D, D], F32, kind="ExternalInput").ap()
    boutr_d = nc.dram_tensor("boutr", [1, D], F32, kind="ExternalInput").ap()
    ident_d = nc.dram_tensor("ident", [128, 128], F32, kind="ExternalInput").ap()

    ctx = ExitStack()
    sb = lambda name, shape, dt: ctx.enter_context(nc.sbuf_tensor(name, shape, dt))
    ps = lambda name, shape, dt: ctx.enter_context(nc.psum_tensor(name, shape, dt))

    NA = DCH * max(FF, S)
    aA = sb("aA", [128, NA], BF16)
    NB = max((FF // 128) * D, NPAIR * S)
    aB = sb("aB", [128, NB], BF16)
    C_xn = [0, 1024]
    C_xnT = [2048, 3072]
    C_hsum = [4096, 4608, 5120, 5632]
    C_hsumT = [6144, 6656, 7168, 7680]
    C_base = [8192, 8704]
    C_xnatt = [9216, 10240]
    C_bblk = 16384
    FFN_C = 20480
    C_QT = 0
    C_KT = 2 * S
    C_V = 4 * S
    C_PT = 4 * S + NKT * 132
    C_sq = C_PT + 2048
    C_raw = C_sq + 512
    C_V2 = C_raw + 512
    ATTN_C = C_V2 + NKT * 132
    aC = sb("aC", [128, max(FFN_C, ATTN_C)], BF16)

    xt_s = sb("xt", [128, 2 * D], F32)
    xo_s = sb("xo", [128, D], F32)
    mv_s = sb("mv", [128, 4], F32)
    bnst_s = sb("bnst", [128, 12], F32)
    rst_s = sb("rst", [128, 4], F32)
    nmr_s = sb("nmr", [128, 2], F32)
    epsc_s = sb("epsc", [128, 1], F32)
    gat_s = sb("gat", [128, 24], F32)
    w_s = sb("wcol", [128, 8], F32)
    llb_s = sb("llb", [128, 16], BF16)
    llT_s = sb("llT", [128, 128], BF16)
    wgA_s = sb("wgA_s", [128, DCH * 20], BF16)
    bgA_s = sb("bgA_s", [1, 20], BF16)
    ident_s = sb("ident_s", [128, 128], BF16)
    onesb_s = sb("onesb", [128, 128], BF16)
    wq_s = sb("wq_s", [128, DCH * 128], BF16)
    wk_s = sb("wk_s", [128, DCH * 128], BF16)
    wv_s = sb("wv_s", [128, DCH * 128], BF16)
    bq8_s = sb("bq8", [128, NPAIR], F32)
    bkp_s = sb("bkp_s", [128, NPAIR], F32)
    bvr_s = sb("bvr_s", [1, 128], BF16)
    boutr_s = sb("boutr_s", [1, D], BF16)
    sqm_s = sb("sqm", [1, 4 * NQB + 8], F32)
    negc_s = sb("negc", [128, 2], F32)
    dnl_s = sb("dnl", [1, 528], F32)
    dnr_s = sb("dnr", [1, 516], BF16)

    PALL = ps("pall", [128, 2048], F32)
    PA = [PALL[:, 0:1024], PALL[:, 1024:2048]]
    PC = [ps(f"pc{i}", [128, 512], F32) for i in range(3)]
    PT_ = ps("pbT", [128, 1024], BF16)

    for s in p.all_sems():
        s.h = ctx.enter_context(nc.semaphore(s.name))

    b_xt = [Buf("xt0"), Buf("xt1")]
    b_xo = [Buf("xo")]
    b_mv = [Buf("mv0"), Buf("mv1")]
    b_bnst = Buf("bnst")
    b_rst = [Buf("rst0"), Buf("rst1")]
    b_nmr = [Buf("nmr0"), Buf("nmr1")]
    b_xn = [Buf("xn0"), Buf("xn1")]
    b_xnT = [Buf("xnT0"), Buf("xnT1")]
    b_gat = Buf("gat")
    b_w = [Buf("w0"), Buf("w1")]
    b_llb = Buf("llb")
    b_llT = Buf("llT")
    b_basebf = [Buf("basebf0"), Buf("basebf1")]
    b_lorabf = Buf("lorabf")
    b_tmp = [Buf("tmp0"), Buf("tmp1")]
    b_hsum = [Buf(f"hsum{i}") for i in range(4)]
    b_hsumT = [Buf(f"hsumT{i}") for i in range(4)]
    b_xna = [Buf("xna0"), Buf("xna1")]
    b_xnat = [Buf(f"xnat{i}") for i in range(NT)]
    b_pa = [[Buf(f"pa{i}_{j}") for j in range(2)] for i in range(2)]
    b_pc = [Buf(f"pc{i}") for i in range(3)]
    b_pT = Buf("pbT")
    b_x1t = [Buf(f"x1t{i}") for i in range(NT)]
    b_x2t = [Buf(f"x2t{i}") for i in range(NT)]
    b_wiT = [[Buf(f"wiT{j}_{k}") for k in range(4)] for j in range(DCH)]
    b_woT = [Buf(f"woT{j}") for j in range(FF // 128)]
    b_wsm = Buf("w_small")
    b_consts = Buf("consts")
    b_wq = Buf("wq")
    b_bvp = Buf("bvp")
    b_wk = Buf("wk")
    b_wv = Buf("wv")
    b_xnTa = [Buf(f"xnTall{i}") for i in range(NT)]
    b_QT = [Buf("QT0"), Buf("QT1")]
    b_KT = [Buf("KT0"), Buf("KT1")]
    b_V = [Buf("V0"), Buf("V1")]
    b_PT = [Buf(f"PT{i}") for i in range(4)]
    b_sq = [Buf("sq0"), Buf("sq1")]
    b_sqm = Buf("sqm")
    b_negc = Buf("negc")
    b_dnl = Buf("dnl")
    b_rdnb = [Buf("rdnb0"), Buf("rdnb1")]
    b_AT = Buf("AT")
    b_woutT = [Buf(f"woutT{j}") for j in range(DCH)]

    def fwiT(j):
        return aA[:, j * FF:(j + 1) * FF]

    def fwoT(j):
        return aB[:, j * D:(j + 1) * D]

    def axnT(j):
        return aA[:, j * S:(j + 1) * S]

    def awoutT(j):
        return aA[:, j * D:(j + 1) * D]

    def aAT(pp):
        return aB[:, pp * S:(pp + 1) * S]

    p.emit("vector", [
        lambda e: e.memset(onesb_s[:], 1.0),
        lambda e: e.memset(epsc_s[:], 1e-5),
    ], writes=[b_consts])

    for dst, src in [(wgA_s[:].rearrange("p (c f) -> p c f", c=DCH),
                      wgA_d.rearrange("(c p) f -> p c f", p=128)),
                     (bgA_s[:], bgA_d),
                     (ident_s[:], ident_d),
                     (boutr_s[:], boutr_d)]:
        p.emit("gpsimd", [lambda e, dst=dst, src=src: e.dma_start(out=dst, in_=src)],
               writes=[b_wsm], dma=True)
    p.emit("sync", [lambda e: e.dma_start(out=bq8_s[:], in_=bqp_d)],
           writes=[b_wsm], dma=True)
    p.emit("sync", [lambda e: e.dma_start(out=bkp_s[:], in_=bkp_d)],
           writes=[b_wsm], dma=True)
    p.emit("vector", [lambda e: e.tensor_scalar(
        bq8_s[:], bq8_s[:], 0.125, None, op0=ALU.mult)], reads=[b_wsm], writes=[b_wsm])

    def load_ffn_weights():
        p.emit("gpsimd", [lambda e: e.dma_start(
            out=aC[:, C_bblk:C_bblk + FF], in_=bblk_d)], writes=[b_wsm], dma=True)
        for k in range(4):
            for j in range(DCH):
                p.emit("gpsimd", [lambda e, j=j, k=k: e.dma_start(
                    out=fwiT(j)[:, k * 1024:(k + 1) * 1024],
                    in_=wiT_d[j * 128:(j + 1) * 128, k * 1024:(k + 1) * 1024])],
                    writes=[b_wiT[j][k]], dma=True)
            for j in range(k * 8, k * 8 + 8):
                p.emit("gpsimd", [lambda e, j=j: e.dma_start(
                    out=fwoT(j), in_=woT_d[j * 128:(j + 1) * 128, :])],
                    writes=[b_woT[j]], dma=True)

    load_ffn_weights()

    def layernorm_tile(par, xt, xn_ap, b_xn_slot):
        mv = mv_s[:, par * 2:par * 2 + 2]
        lnv = rst_s[:, par * 2:par * 2 + 1]
        rstd = rst_s[:, par * 2 + 1:par * 2 + 2]
        nmr = nmr_s[:, par:par + 1]
        p.emit("vector", [
            lambda e: e.bn_stats(bnst_s[:, 0:6], xt[:, 0:512]),
            lambda e: e.bn_stats(bnst_s[:, 6:12], xt[:, 512:1024]),
        ], reads=[b_xt[par]], writes=[b_bnst])
        p.emit("vector", [lambda e: e.bn_aggr(mv, bnst_s[:])],
               reads=[b_bnst], writes=[b_mv[par]])
        p.emit("scalar", [lambda e: e.activation(lnv, mv[:, 1:2], AF.Ln, bias=epsc_s[:])],
               reads=[b_mv[par], b_consts], writes=[b_rst[par]])
        p.emit("scalar", [lambda e: e.activation(rstd, lnv, AF.Exp, scale=-0.5)],
               writes=[b_rst[par]])
        p.emit("vector", [lambda e: e.tensor_scalar(
            nmr, mv[:, 0:1], rstd, -1.0, op0=ALU.mult, op1=ALU.mult)],
            reads=[b_rst[par], b_mv[par]], writes=[b_nmr[par]])
        p.emit("scalar", [lambda e: e.activation(
            xn_ap, xt, AF.Identity, bias=nmr, scale=rstd)],
            reads=[b_xt[par], b_rst[par], b_nmr[par]], writes=[b_xn_slot])

    def transpose_xn(xn, dst_of_j, b_src, b_dst):
        p.emit("scalar", [
            (lambda e, j=j: e.dma_start(out=dst_of_j(j),
                                        in_=xn[:, j * 128:(j + 1) * 128],
                                        transpose=True))
            for j in range(0, DCH, 2)
        ], reads=[b_src], writes=[b_dst], dma=True)
        p.emit("sync", [
            (lambda e, j=j: e.dma_start(out=dst_of_j(j),
                                        in_=xn[:, j * 128:(j + 1) * 128],
                                        transpose=True))
            for j in range(1, DCH, 2)
        ], reads=[b_src], writes=[b_dst], dma=True)

    def layernorm_tile_on(par, xsrc, b_src, xn_ap, b_xn_slot):
        mv = mv_s[:, par * 2:par * 2 + 2]
        lnv = rst_s[:, par * 2:par * 2 + 1]
        rstd = rst_s[:, par * 2 + 1:par * 2 + 2]
        nmr = nmr_s[:, par:par + 1]
        p.emit("vector", [
            lambda e: e.bn_stats(bnst_s[:, 0:6], xsrc[:, 0:512]),
            lambda e: e.bn_stats(bnst_s[:, 6:12], xsrc[:, 512:1024]),
        ], reads=[b_src], writes=[b_bnst])
        p.emit("vector", [lambda e: e.bn_aggr(mv, bnst_s[:])],
               reads=[b_bnst], writes=[b_mv[par]])
        p.emit("scalar", [lambda e: e.activation(lnv, mv[:, 1:2], AF.Ln, bias=epsc_s[:])],
               reads=[b_mv[par], b_consts], writes=[b_rst[par]])
        p.emit("scalar", [lambda e: e.activation(rstd, lnv, AF.Exp, scale=-0.5)],
               writes=[b_rst[par]])
        p.emit("vector", [lambda e: e.tensor_scalar(
            nmr, mv[:, 0:1], rstd, -1.0, op0=ALU.mult, op1=ALU.mult)],
            reads=[b_rst[par], b_mv[par]], writes=[b_nmr[par]])
        p.emit("scalar", [lambda e: e.activation(
            xn_ap, xsrc, AF.Identity, bias=nmr, scale=rstd)],
            reads=[b_src, b_rst[par], b_nmr[par]], writes=[b_xn_slot])

    def ffn_phase(src_d, src_tiles, dst_d, dst_tiles, attn_prep=False):
        backlog = []

        def pop_backlog(n):
            for _ in range(min(n, len(backlog))):
                backlog.pop(0)()

        def load_tile(i):
            par = i % 2
            xt = xt_s[:, par * D:(par + 1) * D]
            if src_tiles is not None and src_tiles[i].ready is not None:
                p.raw_wait("sync", src_tiles[i].ready)
            p.emit("sync", [lambda e, i=i, xt=xt: e.dma_start(
                out=xt, in_=src_d[i * 128:(i + 1) * 128, :])],
                writes=[b_xt[par]], dma=True)

        def ln_tile(i):
            par = i % 2
            xt = xt_s[:, par * D:(par + 1) * D]
            xn = aC[:, C_xn[par]:C_xn[par] + 1024]
            xnT = aC[:, C_xnT[par]:C_xnT[par] + 1024]
            layernorm_tile(par, xt, xn, b_xn[par])
            transpose_xn(xn, lambda j, xnT=xnT: xnT[:, j * 128:(j + 1) * 128],
                         b_xn[par], b_xnT[par])

        for i in range(NT):
            par = i % 2
            xt = xt_s[:, par * D:(par + 1) * D]
            xo = xo_s[:, 0:D]
            xn = aC[:, C_xn[par]:C_xn[par] + 1024]
            xnT = aC[:, C_xnT[par]:C_xnT[par] + 1024]
            wc = w_s[:, par * 4:par * 4 + 4]

            if i == 0:
                load_tile(0)
                ln_tile(0)
                if NT > 1:
                    load_tile(1)
            if i + 1 < NT:
                ln_tile(i + 1)

            def base_mm(fc):
                p.emit("tensor", [
                    (lambda e, j=j, fc=fc, xnT=xnT: e.matmul(
                        PC[0][:, :], xnT[:, j * 128:(j + 1) * 128],
                        fwiT(j)[:, fc * 512:(fc + 1) * 512],
                        start=(j == 0), stop=(j == DCH - 1)))
                    for j in range(DCH)
                ], reads=[b_xnT[par]] + [b_wiT[j][fc // 2] for j in range(DCH)],
                    writes=[b_pc[0]])
                basebf = aC[:, C_base[fc % 2]:C_base[fc % 2] + 512]
                p.emit("scalar", [lambda e, basebf=basebf: e.copy(basebf, PC[0][:, :])],
                       reads=[b_pc[0]], writes=[b_basebf[fc % 2]])

            pg = PA[0][:, 0:20]
            mm = [(lambda e, j=j, xnT=xnT: e.matmul(
                pg, xnT[:, j * 128:(j + 1) * 128], wgA_s[:, j * 20:(j + 1) * 20],
                start=(j == 0), stop=False)) for j in range(DCH)]
            mm.append(lambda e: e.matmul(pg, onesb_s[0:1, :], bgA_s[:],
                                         start=False, stop=True))
            p.emit("tensor", mm, reads=[b_xnT[par], b_wsm, b_consts],
                   writes=[b_pa[0][0]])

            g = gat_s
            p.emit("vector", [lambda e: e.tensor_reduce(
                g[:, 4:5], PA[0][:, 0:4], AX.X, ALU.max, negate=True)],
                reads=[b_pa[0][0]], writes=[b_gat])
            p.emit("scalar", [lambda e: e.activation(
                g[:, 0:4], PA[0][:, 0:4], AF.Exp, bias=g[:, 4:5],
                accum_out=g[:, 5:6])], reads=[b_pa[0][0], b_gat], writes=[b_gat])
            p.emit("vector", [lambda e: e.reciprocal(g[:, 4:5], g[:, 5:6])],
                   writes=[b_gat])
            p.emit("vector", [
                lambda e: e.tensor_scalar(g[:, 4:5], g[:, 4:5], 0.5, None, op0=ALU.mult),
            ], writes=[b_gat])
            p.emit("vector", [
                lambda e: e.tensor_scalar(g[:, 6:10], g[:, 0:4], g[:, 4:5], None,
                                          op0=ALU.mult),
            ], writes=[b_gat])
            rank_fns = []
            for ee in range(E):
                if ee > 0:
                    rank_fns.append(lambda e, ee=ee: e.tensor_scalar(
                        g[:, 18:18 + ee], g[:, 6:6 + ee], g[:, 6 + ee:7 + ee], None,
                        op0=ALU.is_ge))
                    rank_fns.append(lambda e, ee=ee: e.tensor_reduce(
                        g[:, 10 + ee:11 + ee], g[:, 18:18 + ee], AX.X, ALU.add))
                if ee < E - 1:
                    rank_fns.append(lambda e, ee=ee: e.tensor_scalar(
                        g[:, 18:18 + (E - 1 - ee)], g[:, 7 + ee:10],
                        g[:, 6 + ee:7 + ee], None, op0=ALU.is_gt))
                    rank_fns.append(lambda e, ee=ee: e.tensor_reduce(
                        g[:, 22:23], g[:, 18:18 + (E - 1 - ee)], AX.X, ALU.add))
                    if ee > 0:
                        rank_fns.append(lambda e, ee=ee: e.tensor_tensor(
                            g[:, 10 + ee:11 + ee], g[:, 10 + ee:11 + ee],
                            g[:, 22:23], ALU.add))
                    else:
                        rank_fns.append(lambda e: e.tensor_copy(g[:, 10:11], g[:, 22:23]))
            rank_fns.append(lambda e: e.tensor_scalar(
                g[:, 14:18], g[:, 10:14], 1.5, None, op0=ALU.is_lt))
            rank_fns.append(lambda e, wc=wc: e.tensor_tensor(
                wc, g[:, 14:18], g[:, 6:10], ALU.mult))
            for fn in rank_fns:
                p.emit("vector", [fn], writes=[b_gat, b_w[par]])

            p.emit("scalar", [lambda e: e.copy(llb_s[:], PA[0][:, 4:20])],
                   reads=[b_pa[0][0]], writes=[b_llb])
            p.emit("tensor", [lambda e: e.transpose(
                PT_[0:16, 512:640], llb_s[:], ident_s[:])],
                reads=[b_llb, b_wsm], writes=[b_pT])
            p.emit("scalar", [
                (lambda e, ee=ee: e.copy(llT_s[32 * ee:32 * ee + 16, :],
                                         PT_[0:16, 512:640]))
                for ee in range(E)
            ], reads=[b_pT], writes=[b_llT])

            for fc in range(NFC):
                fpar = fc % 2
                f4 = fc % 4
                basebf = aC[:, C_base[fpar]:C_base[fpar] + 512]
                hsum = aC[:, C_hsum[f4]:C_hsum[f4] + 512]
                hsumT = aC[:, C_hsumT[f4]:C_hsumT[f4] + 512]
                base_mm(fc)
                for ee in range(E):
                    p.emit("tensor", [lambda e, ee=ee, fc=fc: e.matmul(
                        PA[ee // 2][:, (ee % 2) * 512:(ee % 2) * 512 + 512],
                        llT_s[32 * ee:32 * ee + 16, :],
                        aC[32 * ee:32 * ee + 16, C_bblk + fc * 512:C_bblk + (fc + 1) * 512],
                        start=True, stop=True, tile_position=(32 * ee, 0))],
                        reads=[b_llT, b_wsm], writes=[b_pa[ee // 2][ee % 2]])
                pop_backlog(3 if fc == 0 else 2)
                if fc == 2 and i + 1 < NT:
                    ln_tile(i + 1)
                tl = 12288 + fpar * 2048
                tlb = aC[:, tl:tl + 2048]
                bl = b_tmp[fpar]
                p.emit("vector", [
                    lambda e, tlb=tlb, basebf=basebf: e.tensor_tensor(
                        tlb.rearrange("p (g f) -> p g f", g=E),
                        PALL[:, :].rearrange("p (g f) -> p g f", g=E),
                        basebf.rearrange("p (g f) -> p g f", g=1).broadcast_to(
                            [128, E, 512]),
                        ALU.add)
                ], reads=[b_basebf[fpar], b_pa[0][0], b_pa[0][1], b_pa[1][0],
                          b_pa[1][1]], writes=[bl])
                p.emit("vector", [
                    (lambda e, ee=ee, tlb=tlb, wc=wc: e.tensor_scalar(
                        tlb[:, ee * 512:(ee + 1) * 512],
                        tlb[:, ee * 512:(ee + 1) * 512],
                        wc[:, ee:ee + 1], 0.0, op0=ALU.mult, op1=ALU.max))
                    for ee in range(E)
                ], reads=[b_w[par]], writes=[bl])
                p.emit("vector", [
                    lambda e, tlb=tlb: e.tensor_tensor(
                        tlb[:, 0:512], tlb[:, 0:512], tlb[:, 512:1024], ALU.add),
                    lambda e, tlb=tlb: e.tensor_tensor(
                        tlb[:, 1024:1536], tlb[:, 1024:1536], tlb[:, 1536:2048],
                        ALU.add),
                ], writes=[bl])
                p.emit("vector", [lambda e, tlb=tlb, hsum=hsum: e.tensor_tensor(
                    hsum, tlb[:, 0:512], tlb[:, 1024:1536], ALU.add)],
                    reads=[bl], writes=[b_hsum[f4]])

                def defer_tail(fc=fc, f4=f4, hsum=hsum, hsumT=hsumT):
                    def t1():
                        p.emit("tensor", [
                            (lambda e, q=q: e.transpose(
                                PT_[:, (f4 % 2) * 512 + q * 128:
                                    (f4 % 2) * 512 + (q + 1) * 128],
                                hsum[:, q * 128:(q + 1) * 128], ident_s[:]))
                            for q in range(4)
                        ], reads=[b_hsum[f4], b_wsm], writes=[b_pT])
                        p.emit("scalar", [lambda e: e.copy(
                            hsumT, PT_[:, (f4 % 2) * 512:(f4 % 2) * 512 + 512])],
                            reads=[b_pT], writes=[b_hsumT[f4]])

                    def t2():
                        p.emit("tensor", [
                            (lambda e, q=q, dm=dm: e.matmul(
                                PC[1 + dm][:, :],
                                hsumT[:, q * 128:(q + 1) * 128],
                                fwoT(fc * 4 + q)[:, dm * 512:(dm + 1) * 512],
                                start=(fc == 0 and q == 0),
                                stop=(fc == NFC - 1 and q == 3)))
                            for dm in range(2) for q in range(4)
                        ], reads=[b_hsumT[f4]] + b_woT,
                            writes=[b_pc[1], b_pc[2]])
                    return [t1, t2]

                backlog.extend(defer_tail())

            def epilogue(i=i, par=par, xo=xo, xt=xt):
                p.emit("vector", [
                    lambda e: e.tensor_tensor(
                        xo[:, 0:512], PC[1][:, :], xt[:, 0:512], ALU.add),
                    lambda e: e.tensor_tensor(
                        xo[:, 512:1024], PC[2][:, :], xt[:, 512:1024], ALU.add),
                ], reads=[b_pc[1], b_pc[2], b_xt[par]], writes=[b_xo[0]])
                p.emit("gpsimd", [lambda e: e.dma_start(
                    out=dst_d[i * 128:(i + 1) * 128, :], in_=xo)],
                    reads=[b_xo[0]],
                    writes=([dst_tiles[i]] if dst_tiles is not None else []),
                    dma=True)
                if attn_prep:
                    xna = aC[:, C_xnatt[par]:C_xnatt[par] + 1024]
                    layernorm_tile_on(par, xo, b_xo[0], xna, b_xna[par])
                    p.emit("gpsimd", [lambda e: e.dma_start(
                        out=xna_d[i * 128:(i + 1) * 128, :], in_=xna)],
                        reads=[b_xna[par]], writes=[b_xnat[i]], dma=True)
                if i + 2 < NT:
                    load_tile(i + 2)

            backlog.append(epilogue)
        pop_backlog(len(backlog))

    ffn_phase(x_in, None, x1_d, b_x1t, attn_prep=(n_phases > 1))
    if n_phases == 1:
        finish(nc, p, ctx)
        return nc
    p.barrier()

    for i in range(NT):
        p.raw_wait("scalar", b_xnat[i].ready)
        p.raw_wait("sync", b_xnat[i].ready)
        p.emit("scalar", [
            (lambda e, j=j, i=i: e.dma_start(
                out=axnT(j)[:, i * 128:(i + 1) * 128],
                in_=xna_d[i * 128:(i + 1) * 128, j * 128:(j + 1) * 128],
                transpose=True))
            for j in range(0, DCH, 2)
        ], writes=[b_xnTa[i]], dma=True)
        p.emit("sync", [
            (lambda e, j=j, i=i: e.dma_start(
                out=axnT(j)[:, i * 128:(i + 1) * 128],
                in_=xna_d[i * 128:(i + 1) * 128, j * 128:(j + 1) * 128],
                transpose=True))
            for j in range(1, DCH, 2)
        ], writes=[b_xnTa[i]], dma=True)

    if ATTN_STAGE == 1:
        finish(nc, p, ctx)
        return nc

    p.emit("vector", [lambda e: e.memset(aC[:, C_V:C_V + NKT * 132], 1.0)],
           writes=[b_V[0]])
    p.emit("vector", [lambda e: e.memset(aC[:, C_V2:C_V2 + NKT * 132], 1.0)],
           writes=[b_V[1]])

    def vbase(pp):
        return C_V if pp % 2 == 0 else C_V2

    def QTb(par):
        return aC[:, C_QT + par * S:C_QT + (par + 1) * S]

    def KTb(par):
        return aC[:, C_KT + par * S:C_KT + (par + 1) * S]

    def load_pair_w(pp):
        p.emit("gpsimd", [lambda e, pp=pp: e.dma_start(
            out=bvr_s[:], in_=bvr_d[0:1, pp * 128:(pp + 1) * 128])],
            writes=[b_bvp], dma=True)
        for wdst, wsrc, bb in [(wq_s, wqp_d, b_wq), (wk_s, wkp_d, b_wk),
                               (wv_s, wvp_d, b_wv)]:
            p.emit("gpsimd", [lambda e, wdst=wdst, wsrc=wsrc, pp=pp: e.dma_start(
                out=wdst[:].rearrange("p (c m) -> p c m", c=DCH),
                in_=wsrc[pp * D:(pp + 1) * D, :].rearrange("(c p) m -> p c m", p=128))],
                writes=[bb], dma=True)

    def qkv_group(pp, kind, qb, bank_ap, bank_buf):
        par = pp % 2
        wsrc = wq_s if kind == 0 else wk_s
        dst = QTb(par) if kind == 0 else KTb(par)
        bdst = b_QT[par] if kind == 0 else b_KT[par]
        bwsrc = b_wq if kind == 0 else b_wk
        bcol = bq8_s[:, pp:pp + 1] if kind == 0 else bkp_s[:, pp:pp + 1]
        scl = 0.125 if kind == 0 else 1.0
        tiles = [b_xnTa[t] for t in range(qb * 4, qb * 4 + 4)]
        p.emit("tensor", [
            (lambda e, j=j, qb=qb, wsrc=wsrc, bank_ap=bank_ap: e.matmul(
                bank_ap, wsrc[:, j * 128:(j + 1) * 128],
                axnT(j)[:, qb * 512:(qb + 1) * 512],
                start=(j == 0), stop=(j == DCH - 1)))
            for j in range(DCH)
        ], reads=[bwsrc] + tiles, writes=[bank_buf])
        p.emit("vector", [lambda e, qb=qb, dst=dst, bank_ap=bank_ap, bcol=bcol,
               scl=scl: e.tensor_scalar(
                   dst[:, qb * 512:(qb + 1) * 512], bank_ap,
                   scl, bcol, op0=ALU.mult, op1=ALU.add)],
               reads=[bank_buf, b_wsm], writes=[bdst])

    def v_group(pp, kt):
        mm = [(lambda e, j=j, kt=kt: e.matmul(
            PC[2][:, 0:128], axnT(j)[:, kt * 128:(kt + 1) * 128],
            wv_s[:, j * 128:(j + 1) * 128],
            start=(j == 0), stop=False)) for j in range(DCH)]
        mm.append(lambda e: e.matmul(
            PC[2][:, 0:128], onesb_s[0:1, :], bvr_s[0:1, :],
            start=False, stop=True))
        p.emit("tensor", mm, reads=[b_wv, b_xnTa[kt], b_consts, b_bvp],
               writes=[b_pc[2]])
        vb = vbase(pp) + kt * 132
        p.emit("vector", [
            lambda e, vb=vb: e.tensor_copy(aC[:, vb:vb + 64], PC[2][:, 0:64]),
            lambda e, vb=vb: e.tensor_copy(aC[:, vb + 66:vb + 130],
                                           PC[2][:, 64:128]),
        ], reads=[b_pc[2]], writes=[b_V[pp % 2]])

    def v_phase(pp):
        for kt in range(NKT):
            v_group(pp, kt)

    def stab_phase(pp):
        par = pp % 2
        QT, KT = QTb(par), KTb(par)
        for kind in range(2):
            src2 = QT if kind == 0 else KT
            bsrc = b_QT[par] if kind == 0 else b_KT[par]
            for qb in range(NQB):
                spar = 0
                sq = aC[:, C_sq:C_sq + 512]
                p.emit("scalar", [lambda e, src2=src2, sq=sq, qb=qb: e.activation(
                    sq, src2[:, qb * 512:(qb + 1) * 512], AF.Square)],
                    reads=[bsrc], writes=[b_sq[spar]])
                for half in range(2):
                    npar = (qb * 2 + half) % 2
                    nbank = PC[2][0:1, 0:512] if npar == 0 else PA[1][0:1, 0:512]
                    nbuf = b_pc[2] if npar == 0 else b_pa[1][0]
                    p.emit("tensor", [lambda e, half=half, sq=sq, nbank=nbank:
                           e.matmul(
                        nbank, onesb_s[half * 64:half * 64 + 64, 0:1],
                        sq[half * 64:half * 64 + 64, :],
                        start=True, stop=True, tile_position=(half * 64, 0))],
                        reads=[b_sq[spar], b_consts], writes=[nbuf])
                    off = 0 if half == 0 else 2 * NQB + 4
                    p.emit("vector", [lambda e, off=off, kind=kind, qb=qb,
                           nbank=nbank: e.tensor_reduce(
                               sqm_s[0:1, off + kind * NQB + qb:
                                     off + kind * NQB + qb + 1],
                               nbank, AX.X, ALU.max)],
                           reads=[nbuf], writes=[b_sqm])
        for half in range(2):
            off = 0 if half == 0 else 2 * NQB + 4
            for kind in range(2):
                p.emit("vector", [lambda e, off=off, kind=kind, half=half:
                       e.tensor_reduce(
                           sqm_s[0:1, 2 * NQB + half * 2 + kind:
                                 2 * NQB + half * 2 + kind + 1],
                           sqm_s[0:1, off + kind * NQB:off + (kind + 1) * NQB],
                           AX.X, ALU.max)],
                       reads=[b_sqm], writes=[b_sqm])
            p.emit("vector", [lambda e, half=half: e.tensor_tensor(
                dnl_s[0:1, half:half + 1],
                sqm_s[0:1, 2 * NQB + half * 2:2 * NQB + half * 2 + 1],
                sqm_s[0:1, 2 * NQB + half * 2 + 1:2 * NQB + half * 2 + 2],
                ALU.mult)], reads=[b_sqm], writes=[b_dnl])
        p.emit("scalar", [lambda e: e.activation(dnl_s[0:1, 2:4], dnl_s[0:1, 0:2],
                                                 AF.Ln)],
               reads=[b_dnl], writes=[b_dnl])
        p.emit("scalar", [lambda e: e.activation(dnl_s[0:1, 4:6], dnl_s[0:1, 2:4],
                                                 AF.Exp, scale=0.5)],
               writes=[b_dnl])
        p.emit("vector", [lambda e: e.tensor_scalar(
            dnr_s[0:1, 512:514], dnl_s[0:1, 4:6], -1.0, None, op0=ALU.mult)],
            reads=[b_dnl], writes=[b_dnl])
        p.emit("tensor", [lambda e: e.matmul(
            PC[2][0:128, 0:2], onesb_s[0:1, :], dnr_s[0:1, 512:514],
            start=True, stop=True)],
            reads=[b_dnl, b_consts], writes=[b_pc[2]])
        p.emit("scalar", [lambda e: e.copy(negc_s[:], PC[2][:, 0:2])],
               reads=[b_pc[2]], writes=[b_negc])

    load_pair_w(0)
    for kind in range(2):
        for qb in range(NQB):
            qkv_group(0, kind, qb, PC[qb % 2][:, :], b_pc[qb % 2])
    v_phase(0)
    stab_phase(0)

    for pp in range(NPAIR):
        par = pp % 2
        QT, KT = QTb(par), KTb(par)
        if pp + 1 < NPAIR:
            load_pair_w(pp + 1)
        qkv_backlog = []
        if pp + 1 < NPAIR:
            for kind in range(2):
                for qb in range(NQB):
                    qkv_backlog.append(
                        lambda kind=kind, qb=qb: qkv_group(
                            pp + 1, kind, qb, PC[2][:, :], b_pc[2]))
            for kt in range(NKT):
                qkv_backlog.append(lambda kt=kt: v_group(pp + 1, kt))

        if ATTN_STAGE == 2:
            continue
        if ATTN_STAGE == 3:
            continue
        for qb in range(NQB):
            abl = []
            for kt in range(NKT):
                sub = kt % 2
                for half in range(2):
                    p.emit("tensor", [lambda e, half=half, kt=kt, qb=qb, sub=sub,
                           QT=QT, KT=KT: e.matmul(
                               PA[half][:, sub * 512:sub * 512 + 512],
                               KT[half * 64:half * 64 + 64,
                                  kt * 128:(kt + 1) * 128],
                               QT[half * 64:half * 64 + 64,
                                  qb * 512:(qb + 1) * 512],
                               start=True, stop=True,
                               tile_position=(half * 64, 0))],
                           reads=[b_QT[par], b_KT[par]],
                           writes=[b_pa[half][sub]])
                for half in range(2):
                    ptb = C_PT + (half * 2 + sub) * 512
                    p.emit("scalar", [lambda e, half=half, ptb=ptb, sub=sub:
                           e.activation(
                        aC[:, ptb:ptb + 512], PA[half][:, sub * 512:sub * 512 + 512],
                        AF.Exp, bias=negc_s[:, half:half + 1])],
                        reads=[b_pa[half][sub], b_negc],
                        writes=[b_PT[half * 2 + sub]])

                    def av(kt=kt, half=half, ptb=ptb, sub=sub, qb=qb):
                        ab = PC[half]
                        bab = b_pc[half]
                        vb = vbase(pp) + kt * 132 + half * 66
                        p.emit("tensor", [lambda e, ab=ab, vb=vb, ptb=ptb, kt=kt:
                               e.matmul(
                                   ab[0:65, :], aC[:, vb:vb + 65],
                                   aC[:, ptb:ptb + 512],
                                   start=(kt == 0), stop=(kt == NKT - 1))],
                               reads=[b_PT[half * 2 + sub], b_V[pp % 2]],
                               writes=[bab])
                    abl.append(av)
                if kt >= 1:
                    abl.pop(0)()
                    abl.pop(0)()
                if qkv_backlog and kt % 2 == 1:
                    qkv_backlog.pop(0)()
            while abl:
                abl.pop(0)()

            for half in range(2):
                ab = PC[half]
                bab = b_pc[half]
                rpar = (qb * 2 + half) % 2
                rawb = aC[rpar * 64:rpar * 64 + 64, C_raw:C_raw + 512]
                p.emit("scalar", [lambda e, ab=ab: e.activation(
                    dnl_s[0:1, 16:528], ab[64:65, :], AF.Ln)],
                    reads=[bab], writes=[b_dnl])
                p.emit("vector", [lambda e, ab=ab, rawb=rawb: e.tensor_copy(
                    rawb, ab[0:64, :])],
                    reads=[bab], writes=[b_rdnb[rpar]])
                p.emit("scalar", [lambda e: e.activation(
                    dnr_s[0:1, 0:512], dnl_s[0:1, 16:528], AF.Exp, scale=-1.0)],
                    writes=[b_dnl])
                rb = PALL[:, 1024 + half * 512:1024 + half * 512 + 512]
                p.emit("tensor", [lambda e, rb=rb: e.matmul(
                    rb, onesb_s[0:1, :], dnr_s[0:1, 0:512],
                    start=True, stop=True)],
                    reads=[b_dnl, b_consts], writes=[b_pa[1][half]])
                p.emit("vector", [lambda e, rb=rb, rawb=rawb, pp=pp, half=half,
                       qb=qb: e.tensor_tensor(
                           aAT(pp)[half * 64:half * 64 + 64,
                                   qb * 512:(qb + 1) * 512],
                           rawb, rb[0:64, :], ALU.mult)],
                       reads=[b_pa[1][half]], writes=[b_AT])
        while qkv_backlog:
            qkv_backlog.pop(0)()
        if pp + 1 < NPAIR:
            stab_phase(pp + 1)

    for j in range(DCH):
        p.emit("gpsimd", [lambda e, j=j: e.dma_start(
            out=awoutT(j), in_=woutT_d[j * 128:(j + 1) * 128, :])],
            writes=[b_woutT[j]] + b_xnTa, dma=True)
    for i in range(NT):
        par = i % 2
        xt = xt_s[:, par * D:(par + 1) * D]
        xo = xo_s[:, 0:D]
        if par == 0:
            banks = [(PC[0][:, :], b_pc[0]), (PC[1][:, :], b_pc[1])]
        else:
            banks = [(PA[0][:, 0:512], b_pa[0][0]), (PA[0][:, 512:1024], b_pa[0][1])]
        p.emit("sync", [lambda e, i=i, xt=xt: e.dma_start(
            out=xt, in_=x1_d[i * 128:(i + 1) * 128, :])],
            writes=[b_xt[par]], dma=True)
        for dm in range(2):
            pbk, bbk = banks[dm]
            mm = [(lambda e, ppx=ppx, i=i, dm=dm, pbk=pbk: e.matmul(
                pbk, aAT(ppx)[:, i * 128:(i + 1) * 128],
                awoutT(ppx)[:, dm * 512:(dm + 1) * 512],
                start=(ppx == 0), stop=False)) for ppx in range(NPAIR)]
            mm.append(lambda e, dm=dm, pbk=pbk: e.matmul(
                pbk, onesb_s[0:1, :], boutr_s[:, dm * 512:(dm + 1) * 512],
                start=False, stop=True))
            p.emit("tensor", mm,
                   reads=[b_AT, b_consts, b_wsm] + b_woutT, writes=[bbk])
        p.emit("vector", [
            lambda e, xo=xo, xt=xt, banks=banks: e.tensor_tensor(
                xo[:, 0:512], banks[0][0], xt[:, 0:512], ALU.add),
            lambda e, xo=xo, xt=xt, banks=banks: e.tensor_tensor(
                xo[:, 512:1024], banks[1][0], xt[:, 512:1024], ALU.add),
        ], reads=[banks[0][1], banks[1][1], b_xt[par]], writes=[b_xo[0]])
        p.emit("sync", [lambda e, i=i, xo=xo: e.dma_start(
            out=x2_d[i * 128:(i + 1) * 128, :], in_=xo)],
            reads=[b_xo[0]], writes=[b_x2t[i]], dma=True)

    if n_phases == 2:
        finish(nc, p, ctx)
        return nc

    def toks_of(bufs):
        out = []
        for b in bufs:
            if b.ready is not None:
                out.append(b.ready)
            out.extend(b.readers)
        return out

    attn_toks = toks_of(
        b_QT + b_KT + b_V + [b_sq[0], b_AT, b_negc, b_sqm, b_dnl] + b_PT + b_rdnb)
    at_toks = toks_of([b_AT] + b_woutT)
    xnta_toks = toks_of(b_xnTa)
    for b in (b_xn + b_xnT + b_hsum + b_hsumT + b_basebf + b_tmp + b_xna + [b_wsm]):
        b.readers.extend(attn_toks)
    for j in range(DCH):
        for k in range(4):
            b_wiT[j][k].readers.extend(xnta_toks)
            if j < 2:
                b_wiT[j][k].readers.extend(at_toks)
    for j in range(FF // 128):
        b_woT[j].readers.extend(at_toks)

    load_ffn_weights()
    ffn_phase(x2_d, b_x2t, y_out, None)

    finish(nc, p, ctx)
    return nc


def finish(nc, p, ctx):
    for q in ("sync", "scalar", "gpsimd"):
        for s in p.dma_pools[q]:
            if s.n > 0:
                p.raw_wait("sync", (s, s.n))
    with nc.Block() as block:
        @block.sync
        def _(e):
            for f in p.ops["sync"]:
                f(e)

        @block.tensor
        def _(e):
            for f in p.ops["tensor"]:
                f(e)

        @block.vector
        def _(e):
            for f in p.ops["vector"]:
                f(e)

        @block.scalar
        def _(e):
            for f in p.ops["scalar"]:
                f(e)

        @block.gpsimd
        def _(e):
            for f in p.ops["gpsimd"]:
                f(e)
    ctx.close()


def kernel(**inputs):
    S = 4096
    x = np.asarray(inputs["L0_attn_x"], np.float32)
    lay = host_layout(inputs)
    nc = build_nc(S)
    in_maps = []
    for c in range(NCORES):
        m = {"x": np.ascontiguousarray(x[c])}
        m.update(lay)
        in_maps.append(m)
    res = run_bass_kernel_spmd(nc, in_maps, core_ids=list(range(NCORES)))
    out = np.stack([res.results[c]["y"] for c in range(NCORES)], axis=0)
    return out.astype(np.float32)


# version 33
# speedup vs baseline: 1.0337x; 1.0036x over previous
import numpy as np
import ml_dtypes
from contextlib import ExitStack

import concourse.bass as bass
import concourse.mybir as mybir
from concourse.bass_utils import run_bass_kernel_spmd

F32 = mybir.dt.float32
BF16 = mybir.dt.bfloat16
AF = mybir.ActivationFunctionType
ALU = mybir.AluOpType
AX = mybir.AxisListType

D = 1024
FF = 4096
E = 4
RK = 4
DH = 64
NPAIR = 8
NCORES = 8
DCH = D // 128

ENGINES = ("sync", "tensor", "vector", "scalar", "gpsimd")
ATTN_STAGE = 0
LOOP_PART = 4


class Sem:
    def __init__(self, name, step):
        self.name, self.step, self.n, self.h = name, step, 0, None


class Buf:
    __slots__ = ("name", "ready", "readers")

    def __init__(self, name):
        self.name, self.ready, self.readers = name, None, []


NDMASEM = 8


class Prog:
    def __init__(self):
        self.ops = {e: [] for e in ENGINES}
        self.sems = {e: Sem("c_" + e, 1) for e in ENGINES}
        self.dma_pools = {q: [Sem(f"d_{q}{i}", 16) for i in range(NDMASEM)]
                          for q in ("sync", "scalar", "gpsimd")}
        self.dma_idx = {q: 0 for q in self.dma_pools}
        self.waited = {e: {} for e in ENGINES}

    def all_sems(self):
        out = list(self.sems.values())
        for pool in self.dma_pools.values():
            out.extend(pool)
        return out

    def _wait(self, eng, tok):
        if tok is None:
            return
        if isinstance(tok, list):
            for t in tok:
                self._wait(eng, t)
            return
        sem, val = tok
        if self.waited[eng].get(sem.name, 0) >= val:
            return
        self.waited[eng][sem.name] = val
        self.ops[eng].append(lambda ee, s=sem, v=val: ee.wait_ge(s.h, v))

    def emit(self, eng, fns, reads=(), writes=(), dma=False):
        for b in reads:
            self._wait(eng, b.ready)
        for b in writes:
            self._wait(eng, b.ready)
            for t in b.readers:
                self._wait(eng, t)
        if dma:
            toks = []
            for fn in fns:
                pool = self.dma_pools[eng]
                sem = pool[self.dma_idx[eng]]
                self.dma_idx[eng] = (self.dma_idx[eng] + 1) % NDMASEM
                if sem.n > 0:
                    self._wait(eng, (sem, sem.n))
                sem.n += 16
                self.ops[eng].append(
                    lambda ee, f=fn, s=sem: f(ee).then_inc(s.h, 16))
                toks.append((sem, sem.n))
            tok = toks if len(toks) > 1 else toks[0]
        else:
            sem = self.sems[eng]
            n = len(fns)
            for i, fn in enumerate(fns):
                if i == n - 1:
                    sem.n += 1
                    self.ops[eng].append(
                        lambda ee, f=fn, s=sem: f(ee).then_inc(s.h, 1))
                else:
                    self.ops[eng].append(lambda ee, f=fn: f(ee))
            tok = (sem, sem.n)
        for b in reads:
            b.readers.append(tok)
        for b in writes:
            b.ready = tok
            b.readers = []
        return tok

    def raw_wait(self, eng, tok):
        self._wait(eng, tok)

    def barrier(self):
        toks = [(s, s.n) for s in self.all_sems() if s.n > 0]
        for e in ENGINES:
            own = self.sems[e]
            for t in toks:
                if t[0] is not own:
                    self._wait(e, t)


def host_layout(inputs):
    W_qkv = np.asarray(inputs["L0_attn_W_qkv"], np.float32)
    b_qkv = np.asarray(inputs["L0_attn_b_qkv"], np.float32)
    W_out = np.asarray(inputs["L0_attn_W_out"], np.float32)
    b_out = np.asarray(inputs["L0_attn_b_out"], np.float32)
    Wg = np.asarray(inputs["L0_ffn_Wg"], np.float32)
    bg = np.asarray(inputs["L0_ffn_bg"], np.float32)
    Wi = np.asarray(inputs["L0_ffn_Wi"], np.float32)
    Wo = np.asarray(inputs["L0_ffn_Wo"], np.float32)
    A = np.asarray(inputs["L0_ffn_A"], np.float32)
    B = np.asarray(inputs["L0_ffn_B"], np.float32)

    out = {}
    out["wiT"] = np.ascontiguousarray(Wi.T)
    out["woT"] = np.ascontiguousarray(Wo.T)
    out["wgA"] = np.ascontiguousarray(
        np.concatenate([Wg.T, A.reshape(E * RK, D).T], axis=1))
    out["bgA"] = np.ascontiguousarray(
        np.concatenate([bg, np.zeros(E * RK, np.float32)])[None, :])
    bblk = np.zeros((128, FF), np.float32)
    for e in range(E):
        for r in range(RK):
            bblk[32 * e + 4 * e + r, :] = B[e, :, r]
    out["bblk"] = bblk
    wq = np.zeros((NPAIR * D, 128), np.float32)
    wk = np.zeros((NPAIR * D, 128), np.float32)
    wv = np.zeros((NPAIR * D, 128), np.float32)
    bq = np.zeros((128, NPAIR), np.float32)
    bk = np.zeros((128, NPAIR), np.float32)
    bv = np.zeros((1, D), np.float32)
    for p_ in range(NPAIR):
        for j in range(2):
            h = 2 * p_ + j
            base = h * 3 * DH
            wq[p_ * D:(p_ + 1) * D, j * DH:(j + 1) * DH] = W_qkv[base:base + DH, :].T
            wk[p_ * D:(p_ + 1) * D, j * DH:(j + 1) * DH] = W_qkv[base + DH:base + 2 * DH, :].T
            wv[p_ * D:(p_ + 1) * D, j * DH:(j + 1) * DH] = W_qkv[base + 2 * DH:base + 3 * DH, :].T
            bq[j * DH:(j + 1) * DH, p_] = b_qkv[base:base + DH]
            bk[j * DH:(j + 1) * DH, p_] = b_qkv[base + DH:base + 2 * DH]
            bv[0, p_ * 128 + j * DH:p_ * 128 + (j + 1) * DH] = \
                b_qkv[base + 2 * DH:base + 3 * DH]
    out["wqp"], out["wkp"], out["wvp"] = wq, wk, wv
    out["bqp"], out["bkp"], out["bvr"] = bq, bk, bv
    out["woutT"] = np.ascontiguousarray(W_out.T)
    out["boutr"] = np.ascontiguousarray(b_out[None, :])
    out["ident"] = np.eye(128, dtype=np.float32)
    return out


def build_nc(S, n_phases=3):
    NT = S // 128
    NFC = FF // 512
    NQB = S // 512
    NKT = S // 128

    nc = bass.Bass("TRN2", target_bir_lowering=False)
    p = Prog()

    x_in = nc.dram_tensor("x", [S, D], F32, kind="ExternalInput").ap()
    y_out = nc.dram_tensor("y", [S, D], F32, kind="ExternalOutput").ap()
    x1_d = nc.dram_tensor("x1_scratch", [S, D], F32, kind="Internal").ap()
    x2_d = nc.dram_tensor("x2_scratch", [S, D], F32, kind="Internal").ap()
    xna_d = nc.dram_tensor("xna_scratch", [S, D], BF16, kind="Internal").ap()
    wiT_d = nc.dram_tensor("wiT", [D, FF], F32, kind="ExternalInput").ap()
    woT_d = nc.dram_tensor("woT", [FF, D], F32, kind="ExternalInput").ap()
    wgA_d = nc.dram_tensor("wgA", [D, 20], F32, kind="ExternalInput").ap()
    bgA_d = nc.dram_tensor("bgA", [1, 20], F32, kind="ExternalInput").ap()
    bblk_d = nc.dram_tensor("bblk", [128, FF], F32, kind="ExternalInput").ap()
    wqp_d = nc.dram_tensor("wqp", [NPAIR * D, 128], F32, kind="ExternalInput").ap()
    wkp_d = nc.dram_tensor("wkp", [NPAIR * D, 128], F32, kind="ExternalInput").ap()
    wvp_d = nc.dram_tensor("wvp", [NPAIR * D, 128], F32, kind="ExternalInput").ap()
    bqp_d = nc.dram_tensor("bqp", [128, NPAIR], F32, kind="ExternalInput").ap()
    bkp_d = nc.dram_tensor("bkp", [128, NPAIR], F32, kind="ExternalInput").ap()
    bvr_d = nc.dram_tensor("bvr", [1, D], F32, kind="ExternalInput").ap()
    woutT_d = nc.dram_tensor("woutT", [D, D], F32, kind="ExternalInput").ap()
    boutr_d = nc.dram_tensor("boutr", [1, D], F32, kind="ExternalInput").ap()
    ident_d = nc.dram_tensor("ident", [128, 128], F32, kind="ExternalInput").ap()

    ctx = ExitStack()
    sb = lambda name, shape, dt: ctx.enter_context(nc.sbuf_tensor(name, shape, dt))
    ps = lambda name, shape, dt: ctx.enter_context(nc.psum_tensor(name, shape, dt))

    NA = DCH * max(FF, S)
    aA = sb("aA", [128, NA], BF16)
    NB = max((FF // 128) * D, NPAIR * S)
    aB = sb("aB", [128, NB], BF16)
    C_xn = [0, 1024]
    C_xnT = [2048, 3072]
    C_hsum = [4096, 4608, 5120, 5632]
    C_hsumT = [6144, 6656, 7168, 7680]
    C_base = [8192, 8704]
    C_xnatt = [9216, 10240]
    C_bblk = 16384
    FFN_C = 20480
    C_QT = 0
    C_KT = 2 * S
    C_V = 4 * S
    C_PT = 4 * S + NKT * 132
    C_sq = C_PT + 2048
    C_raw = C_sq + 512
    C_V2 = C_raw + 512
    ATTN_C = C_V2 + NKT * 132
    aC = sb("aC", [128, max(FFN_C, ATTN_C)], BF16)

    xt_s = sb("xt", [128, 2 * D], F32)
    xo_s = sb("xo", [128, D], F32)
    mv_s = sb("mv", [128, 4], F32)
    bnst_s = sb("bnst", [128, 12], F32)
    rst_s = sb("rst", [128, 4], F32)
    nmr_s = sb("nmr", [128, 2], F32)
    epsc_s = sb("epsc", [128, 1], F32)
    gat_s = sb("gat", [128, 24], F32)
    w_s = sb("wcol", [128, 8], F32)
    llb_s = sb("llb", [128, 16], BF16)
    llT_s = sb("llT", [128, 128], BF16)
    wgA_s = sb("wgA_s", [128, DCH * 20], BF16)
    bgA_s = sb("bgA_s", [1, 20], BF16)
    ident_s = sb("ident_s", [128, 128], BF16)
    onesb_s = sb("onesb", [128, 128], BF16)
    wq_s = sb("wq_s", [128, DCH * 128], BF16)
    wk_s = sb("wk_s", [128, DCH * 128], BF16)
    wv_s = sb("wv_s", [128, DCH * 128], BF16)
    bq8_s = sb("bq8", [128, NPAIR], F32)
    bkp_s = sb("bkp_s", [128, NPAIR], F32)
    bvr_s = sb("bvr_s", [1, 128], BF16)
    boutr_s = sb("boutr_s", [1, D], BF16)
    sqm_s = sb("sqm", [1, 4 * NQB + 8], F32)
    negc_s = sb("negc", [128, 2], F32)
    dnl_s = sb("dnl", [1, 528], F32)
    dnr_s = sb("dnr", [1, 516], BF16)

    PALL = ps("pall", [128, 2048], F32)
    PA = [PALL[:, 0:1024], PALL[:, 1024:2048]]
    PC = [ps(f"pc{i}", [128, 512], F32) for i in range(3)]
    PT_ = ps("pbT", [128, 1024], BF16)

    for s in p.all_sems():
        s.h = ctx.enter_context(nc.semaphore(s.name))

    b_xt = [Buf("xt0"), Buf("xt1")]
    b_xo = [Buf("xo")]
    b_mv = [Buf("mv0"), Buf("mv1")]
    b_bnst = Buf("bnst")
    b_rst = [Buf("rst0"), Buf("rst1")]
    b_nmr = [Buf("nmr0"), Buf("nmr1")]
    b_xn = [Buf("xn0"), Buf("xn1")]
    b_xnT = [Buf("xnT0"), Buf("xnT1")]
    b_gat = Buf("gat")
    b_w = [Buf("w0"), Buf("w1")]
    b_llb = Buf("llb")
    b_llT = Buf("llT")
    b_basebf = [Buf("basebf0"), Buf("basebf1")]
    b_lorabf = Buf("lorabf")
    b_tmp = [Buf("tmp0"), Buf("tmp1")]
    b_hsum = [Buf(f"hsum{i}") for i in range(4)]
    b_hsumT = [Buf(f"hsumT{i}") for i in range(4)]
    b_xna = [Buf("xna0"), Buf("xna1")]
    b_xnat = [Buf(f"xnat{i}") for i in range(NT)]
    b_pa = [[Buf(f"pa{i}_{j}") for j in range(2)] for i in range(2)]
    b_pc = [Buf(f"pc{i}") for i in range(3)]
    b_pT = Buf("pbT")
    b_x1t = [Buf(f"x1t{i}") for i in range(NT)]
    b_x2t = [Buf(f"x2t{i}") for i in range(NT)]
    b_wiT = [[Buf(f"wiT{j}_{k}") for k in range(4)] for j in range(DCH)]
    b_woT = [Buf(f"woT{j}") for j in range(FF // 128)]
    b_wsm = Buf("w_small")
    b_consts = Buf("consts")
    b_wq = Buf("wq")
    b_bvp = Buf("bvp")
    b_wk = Buf("wk")
    b_wv = Buf("wv")
    b_xnTa = [Buf(f"xnTall{i}") for i in range(NT)]
    b_QT = [Buf("QT0"), Buf("QT1")]
    b_KT = [Buf("KT0"), Buf("KT1")]
    b_V = [Buf("V0"), Buf("V1")]
    b_PT = [Buf(f"PT{i}") for i in range(4)]
    b_sq = [Buf("sq0"), Buf("sq1")]
    b_sqm = Buf("sqm")
    b_negc = Buf("negc")
    b_dnl = Buf("dnl")
    b_rdnb = [Buf("rdnb0"), Buf("rdnb1")]
    b_AT = Buf("AT")
    b_woutT = [Buf(f"woutT{j}") for j in range(DCH)]

    def fwiT(j):
        return aA[:, j * FF:(j + 1) * FF]

    def fwoT(j):
        return aB[:, j * D:(j + 1) * D]

    def axnT(j):
        return aA[:, j * S:(j + 1) * S]

    def awoutT(j):
        return aA[:, j * D:(j + 1) * D]

    def aAT(pp):
        return aB[:, pp * S:(pp + 1) * S]

    p.emit("vector", [
        lambda e: e.memset(onesb_s[:], 1.0),
        lambda e: e.memset(epsc_s[:], 1e-5),
    ], writes=[b_consts])

    for dst, src in [(wgA_s[:].rearrange("p (c f) -> p c f", c=DCH),
                      wgA_d.rearrange("(c p) f -> p c f", p=128)),
                     (bgA_s[:], bgA_d),
                     (ident_s[:], ident_d),
                     (boutr_s[:], boutr_d)]:
        p.emit("gpsimd", [lambda e, dst=dst, src=src: e.dma_start(out=dst, in_=src)],
               writes=[b_wsm], dma=True)
    p.emit("sync", [lambda e: e.dma_start(out=bq8_s[:], in_=bqp_d)],
           writes=[b_wsm], dma=True)
    p.emit("sync", [lambda e: e.dma_start(out=bkp_s[:], in_=bkp_d)],
           writes=[b_wsm], dma=True)
    p.emit("vector", [lambda e: e.tensor_scalar(
        bq8_s[:], bq8_s[:], 0.125, None, op0=ALU.mult)], reads=[b_wsm], writes=[b_wsm])

    def load_ffn_weights():
        p.emit("gpsimd", [lambda e: e.dma_start(
            out=aC[:, C_bblk:C_bblk + FF], in_=bblk_d)], writes=[b_wsm], dma=True)
        for k in range(4):
            for j in range(DCH):
                p.emit("gpsimd", [lambda e, j=j, k=k: e.dma_start(
                    out=fwiT(j)[:, k * 1024:(k + 1) * 1024],
                    in_=wiT_d[j * 128:(j + 1) * 128, k * 1024:(k + 1) * 1024])],
                    writes=[b_wiT[j][k]], dma=True)
            for j in range(k * 8, k * 8 + 8):
                p.emit("gpsimd", [lambda e, j=j: e.dma_start(
                    out=fwoT(j), in_=woT_d[j * 128:(j + 1) * 128, :])],
                    writes=[b_woT[j]], dma=True)

    load_ffn_weights()

    def layernorm_tile(par, xt, xn_ap, b_xn_slot):
        mv = mv_s[:, par * 2:par * 2 + 2]
        lnv = rst_s[:, par * 2:par * 2 + 1]
        rstd = rst_s[:, par * 2 + 1:par * 2 + 2]
        nmr = nmr_s[:, par:par + 1]
        p.emit("vector", [
            lambda e: e.bn_stats(bnst_s[:, 0:6], xt[:, 0:512]),
            lambda e: e.bn_stats(bnst_s[:, 6:12], xt[:, 512:1024]),
        ], reads=[b_xt[par]], writes=[b_bnst])
        p.emit("vector", [lambda e: e.bn_aggr(mv, bnst_s[:])],
               reads=[b_bnst], writes=[b_mv[par]])
        p.emit("scalar", [lambda e: e.activation(lnv, mv[:, 1:2], AF.Ln, bias=epsc_s[:])],
               reads=[b_mv[par], b_consts], writes=[b_rst[par]])
        p.emit("scalar", [lambda e: e.activation(rstd, lnv, AF.Exp, scale=-0.5)],
               writes=[b_rst[par]])
        p.emit("vector", [lambda e: e.tensor_scalar(
            nmr, mv[:, 0:1], rstd, -1.0, op0=ALU.mult, op1=ALU.mult)],
            reads=[b_rst[par], b_mv[par]], writes=[b_nmr[par]])
        p.emit("scalar", [lambda e: e.activation(
            xn_ap, xt, AF.Identity, bias=nmr, scale=rstd)],
            reads=[b_xt[par], b_rst[par], b_nmr[par]], writes=[b_xn_slot])

    def transpose_xn(xn, dst_of_j, b_src, b_dst):
        p.emit("scalar", [
            (lambda e, j=j: e.dma_start(out=dst_of_j(j),
                                        in_=xn[:, j * 128:(j + 1) * 128],
                                        transpose=True))
            for j in range(0, DCH, 2)
        ], reads=[b_src], writes=[b_dst], dma=True)
        p.emit("sync", [
            (lambda e, j=j: e.dma_start(out=dst_of_j(j),
                                        in_=xn[:, j * 128:(j + 1) * 128],
                                        transpose=True))
            for j in range(1, DCH, 2)
        ], reads=[b_src], writes=[b_dst], dma=True)

    def layernorm_tile_on(par, xsrc, b_src, xn_ap, b_xn_slot):
        mv = mv_s[:, par * 2:par * 2 + 2]
        lnv = rst_s[:, par * 2:par * 2 + 1]
        rstd = rst_s[:, par * 2 + 1:par * 2 + 2]
        nmr = nmr_s[:, par:par + 1]
        p.emit("vector", [
            lambda e: e.bn_stats(bnst_s[:, 0:6], xsrc[:, 0:512]),
            lambda e: e.bn_stats(bnst_s[:, 6:12], xsrc[:, 512:1024]),
        ], reads=[b_src], writes=[b_bnst])
        p.emit("vector", [lambda e: e.bn_aggr(mv, bnst_s[:])],
               reads=[b_bnst], writes=[b_mv[par]])
        p.emit("scalar", [lambda e: e.activation(lnv, mv[:, 1:2], AF.Ln, bias=epsc_s[:])],
               reads=[b_mv[par], b_consts], writes=[b_rst[par]])
        p.emit("scalar", [lambda e: e.activation(rstd, lnv, AF.Exp, scale=-0.5)],
               writes=[b_rst[par]])
        p.emit("vector", [lambda e: e.tensor_scalar(
            nmr, mv[:, 0:1], rstd, -1.0, op0=ALU.mult, op1=ALU.mult)],
            reads=[b_rst[par], b_mv[par]], writes=[b_nmr[par]])
        p.emit("scalar", [lambda e: e.activation(
            xn_ap, xsrc, AF.Identity, bias=nmr, scale=rstd)],
            reads=[b_src, b_rst[par], b_nmr[par]], writes=[b_xn_slot])

    def ffn_phase(src_d, src_tiles, dst_d, dst_tiles, attn_prep=False):
        backlog = []

        def pop_backlog(n):
            for _ in range(min(n, len(backlog))):
                backlog.pop(0)()

        def load_tile(i):
            par = i % 2
            xt = xt_s[:, par * D:(par + 1) * D]
            if src_tiles is not None and src_tiles[i].ready is not None:
                p.raw_wait("sync", src_tiles[i].ready)
            p.emit("sync", [lambda e, i=i, xt=xt: e.dma_start(
                out=xt, in_=src_d[i * 128:(i + 1) * 128, :])],
                writes=[b_xt[par]], dma=True)

        def ln_tile(i):
            par = i % 2
            xt = xt_s[:, par * D:(par + 1) * D]
            xn = aC[:, C_xn[par]:C_xn[par] + 1024]
            xnT = aC[:, C_xnT[par]:C_xnT[par] + 1024]
            layernorm_tile(par, xt, xn, b_xn[par])
            transpose_xn(xn, lambda j, xnT=xnT: xnT[:, j * 128:(j + 1) * 128],
                         b_xn[par], b_xnT[par])

        for i in range(NT):
            par = i % 2
            xt = xt_s[:, par * D:(par + 1) * D]
            xo = xo_s[:, 0:D]
            xn = aC[:, C_xn[par]:C_xn[par] + 1024]
            xnT = aC[:, C_xnT[par]:C_xnT[par] + 1024]
            wc = w_s[:, par * 4:par * 4 + 4]

            if i == 0:
                load_tile(0)
                ln_tile(0)
                if NT > 1:
                    load_tile(1)
            if i + 1 < NT:
                ln_tile(i + 1)

            def base_mm(fc):
                p.emit("tensor", [
                    (lambda e, j=j, fc=fc, xnT=xnT: e.matmul(
                        PC[0][:, :], xnT[:, j * 128:(j + 1) * 128],
                        fwiT(j)[:, fc * 512:(fc + 1) * 512],
                        start=(j == 0), stop=(j == DCH - 1)))
                    for j in range(DCH)
                ], reads=[b_xnT[par]] + [b_wiT[j][fc // 2] for j in range(DCH)],
                    writes=[b_pc[0]])
                basebf = aC[:, C_base[fc % 2]:C_base[fc % 2] + 512]
                p.emit("scalar", [lambda e, basebf=basebf: e.copy(basebf, PC[0][:, :])],
                       reads=[b_pc[0]], writes=[b_basebf[fc % 2]])

            pg = PA[0][:, 0:20]
            mm = [(lambda e, j=j, xnT=xnT: e.matmul(
                pg, xnT[:, j * 128:(j + 1) * 128], wgA_s[:, j * 20:(j + 1) * 20],
                start=(j == 0), stop=False)) for j in range(DCH)]
            mm.append(lambda e: e.matmul(pg, onesb_s[0:1, :], bgA_s[:],
                                         start=False, stop=True))
            p.emit("tensor", mm, reads=[b_xnT[par], b_wsm, b_consts],
                   writes=[b_pa[0][0]])

            g = gat_s
            p.emit("vector", [lambda e: e.tensor_reduce(
                g[:, 4:5], PA[0][:, 0:4], AX.X, ALU.max, negate=True)],
                reads=[b_pa[0][0]], writes=[b_gat])
            p.emit("scalar", [lambda e: e.activation(
                g[:, 0:4], PA[0][:, 0:4], AF.Exp, bias=g[:, 4:5],
                accum_out=g[:, 5:6])], reads=[b_pa[0][0], b_gat], writes=[b_gat])
            p.emit("vector", [lambda e: e.reciprocal(g[:, 4:5], g[:, 5:6])],
                   writes=[b_gat])
            p.emit("vector", [
                lambda e: e.tensor_scalar(g[:, 4:5], g[:, 4:5], 0.5, None, op0=ALU.mult),
            ], writes=[b_gat])
            p.emit("vector", [
                lambda e: e.tensor_scalar(g[:, 6:10], g[:, 0:4], g[:, 4:5], None,
                                          op0=ALU.mult),
            ], writes=[b_gat])
            rank_fns = []
            for ee in range(E):
                if ee > 0:
                    rank_fns.append(lambda e, ee=ee: e.tensor_scalar(
                        g[:, 18:18 + ee], g[:, 6:6 + ee], g[:, 6 + ee:7 + ee], None,
                        op0=ALU.is_ge))
                    rank_fns.append(lambda e, ee=ee: e.tensor_reduce(
                        g[:, 10 + ee:11 + ee], g[:, 18:18 + ee], AX.X, ALU.add))
                if ee < E - 1:
                    rank_fns.append(lambda e, ee=ee: e.tensor_scalar(
                        g[:, 18:18 + (E - 1 - ee)], g[:, 7 + ee:10],
                        g[:, 6 + ee:7 + ee], None, op0=ALU.is_gt))
                    rank_fns.append(lambda e, ee=ee: e.tensor_reduce(
                        g[:, 22:23], g[:, 18:18 + (E - 1 - ee)], AX.X, ALU.add))
                    if ee > 0:
                        rank_fns.append(lambda e, ee=ee: e.tensor_tensor(
                            g[:, 10 + ee:11 + ee], g[:, 10 + ee:11 + ee],
                            g[:, 22:23], ALU.add))
                    else:
                        rank_fns.append(lambda e: e.tensor_copy(g[:, 10:11], g[:, 22:23]))
            rank_fns.append(lambda e: e.tensor_scalar(
                g[:, 14:18], g[:, 10:14], 1.5, None, op0=ALU.is_lt))
            rank_fns.append(lambda e, wc=wc: e.tensor_tensor(
                wc, g[:, 14:18], g[:, 6:10], ALU.mult))
            for fn in rank_fns:
                p.emit("vector", [fn], writes=[b_gat, b_w[par]])

            p.emit("scalar", [lambda e: e.copy(llb_s[:], PA[0][:, 4:20])],
                   reads=[b_pa[0][0]], writes=[b_llb])
            p.emit("tensor", [lambda e: e.transpose(
                PT_[0:16, 512:640], llb_s[:], ident_s[:])],
                reads=[b_llb, b_wsm], writes=[b_pT])
            p.emit("scalar", [
                (lambda e, ee=ee: e.copy(llT_s[32 * ee:32 * ee + 16, :],
                                         PT_[0:16, 512:640]))
                for ee in range(E)
            ], reads=[b_pT], writes=[b_llT])

            for fc in range(NFC):
                fpar = fc % 2
                f4 = fc % 4
                basebf = aC[:, C_base[fpar]:C_base[fpar] + 512]
                hsum = aC[:, C_hsum[f4]:C_hsum[f4] + 512]
                hsumT = aC[:, C_hsumT[f4]:C_hsumT[f4] + 512]
                base_mm(fc)
                for ee in range(E):
                    p.emit("tensor", [lambda e, ee=ee, fc=fc: e.matmul(
                        PA[ee // 2][:, (ee % 2) * 512:(ee % 2) * 512 + 512],
                        llT_s[32 * ee:32 * ee + 16, :],
                        aC[32 * ee:32 * ee + 16, C_bblk + fc * 512:C_bblk + (fc + 1) * 512],
                        start=True, stop=True, tile_position=(32 * ee, 0))],
                        reads=[b_llT, b_wsm], writes=[b_pa[ee // 2][ee % 2]])
                pop_backlog(3 if fc == 0 else 2)
                if fc == 2 and i + 1 < NT:
                    ln_tile(i + 1)
                tl = 12288 + fpar * 2048
                tlb = aC[:, tl:tl + 2048]
                bl = b_tmp[fpar]
                p.emit("vector", [
                    lambda e, tlb=tlb, basebf=basebf: e.tensor_tensor(
                        tlb.rearrange("p (g f) -> p g f", g=E),
                        PALL[:, :].rearrange("p (g f) -> p g f", g=E),
                        basebf.rearrange("p (g f) -> p g f", g=1).broadcast_to(
                            [128, E, 512]),
                        ALU.add)
                ], reads=[b_basebf[fpar], b_pa[0][0], b_pa[0][1], b_pa[1][0],
                          b_pa[1][1]], writes=[bl])
                p.emit("vector", [
                    (lambda e, ee=ee, tlb=tlb, wc=wc: e.tensor_scalar(
                        tlb[:, ee * 512:(ee + 1) * 512],
                        tlb[:, ee * 512:(ee + 1) * 512],
                        wc[:, ee:ee + 1], 0.0, op0=ALU.mult, op1=ALU.max))
                    for ee in range(E)
                ], reads=[b_w[par]], writes=[bl])
                p.emit("vector", [
                    lambda e, tlb=tlb: e.tensor_tensor(
                        tlb[:, 0:512], tlb[:, 0:512], tlb[:, 512:1024], ALU.add),
                    lambda e, tlb=tlb: e.tensor_tensor(
                        tlb[:, 1024:1536], tlb[:, 1024:1536], tlb[:, 1536:2048],
                        ALU.add),
                ], writes=[bl])
                p.emit("vector", [lambda e, tlb=tlb, hsum=hsum: e.tensor_tensor(
                    hsum, tlb[:, 0:512], tlb[:, 1024:1536], ALU.add)],
                    reads=[bl], writes=[b_hsum[f4]])

                def defer_tail(fc=fc, f4=f4, hsum=hsum, hsumT=hsumT):
                    def t1():
                        p.emit("tensor", [
                            (lambda e, q=q: e.transpose(
                                PT_[:, (f4 % 2) * 512 + q * 128:
                                    (f4 % 2) * 512 + (q + 1) * 128],
                                hsum[:, q * 128:(q + 1) * 128], ident_s[:]))
                            for q in range(4)
                        ], reads=[b_hsum[f4], b_wsm], writes=[b_pT])
                        p.emit("scalar", [lambda e: e.copy(
                            hsumT, PT_[:, (f4 % 2) * 512:(f4 % 2) * 512 + 512])],
                            reads=[b_pT], writes=[b_hsumT[f4]])

                    def t2():
                        p.emit("tensor", [
                            (lambda e, q=q, dm=dm: e.matmul(
                                PC[1 + dm][:, :],
                                hsumT[:, q * 128:(q + 1) * 128],
                                fwoT(fc * 4 + q)[:, dm * 512:(dm + 1) * 512],
                                start=(fc == 0 and q == 0),
                                stop=(fc == NFC - 1 and q == 3)))
                            for dm in range(2) for q in range(4)
                        ], reads=[b_hsumT[f4]] + b_woT,
                            writes=[b_pc[1], b_pc[2]])
                    return [t1, t2]

                backlog.extend(defer_tail())

            def epilogue(i=i, par=par, xo=xo, xt=xt):
                p.emit("vector", [
                    lambda e: e.tensor_tensor(
                        xo[:, 0:512], PC[1][:, :], xt[:, 0:512], ALU.add),
                    lambda e: e.tensor_tensor(
                        xo[:, 512:1024], PC[2][:, :], xt[:, 512:1024], ALU.add),
                ], reads=[b_pc[1], b_pc[2], b_xt[par]], writes=[b_xo[0]])
                p.emit("gpsimd", [lambda e: e.dma_start(
                    out=dst_d[i * 128:(i + 1) * 128, :], in_=xo)],
                    reads=[b_xo[0]],
                    writes=([dst_tiles[i]] if dst_tiles is not None else []),
                    dma=True)
                if attn_prep:
                    xna = aC[:, C_xnatt[par]:C_xnatt[par] + 1024]
                    layernorm_tile_on(par, xo, b_xo[0], xna, b_xna[par])
                    p.emit("gpsimd", [lambda e: e.dma_start(
                        out=xna_d[i * 128:(i + 1) * 128, :], in_=xna)],
                        reads=[b_xna[par]], writes=[b_xnat[i]], dma=True)
                if i + 2 < NT:
                    load_tile(i + 2)

            backlog.append(epilogue)
        pop_backlog(len(backlog))

    ffn_phase(x_in, None, x1_d, b_x1t, attn_prep=(n_phases > 1))
    if n_phases == 1:
        finish(nc, p, ctx)
        return nc
    p.barrier()

    for i in range(NT):
        p.raw_wait("scalar", b_xnat[i].ready)
        p.raw_wait("sync", b_xnat[i].ready)
        p.emit("scalar", [
            (lambda e, j=j, i=i: e.dma_start(
                out=axnT(j)[:, i * 128:(i + 1) * 128],
                in_=xna_d[i * 128:(i + 1) * 128, j * 128:(j + 1) * 128],
                transpose=True))
            for j in range(0, DCH, 2)
        ], writes=[b_xnTa[i]], dma=True)
        p.emit("sync", [
            (lambda e, j=j, i=i: e.dma_start(
                out=axnT(j)[:, i * 128:(i + 1) * 128],
                in_=xna_d[i * 128:(i + 1) * 128, j * 128:(j + 1) * 128],
                transpose=True))
            for j in range(1, DCH, 2)
        ], writes=[b_xnTa[i]], dma=True)

    if ATTN_STAGE == 1:
        finish(nc, p, ctx)
        return nc

    p.emit("vector", [lambda e: e.memset(aC[:, C_V:C_V + NKT * 132], 1.0)],
           writes=[b_V[0]])
    p.emit("vector", [lambda e: e.memset(aC[:, C_V2:C_V2 + NKT * 132], 1.0)],
           writes=[b_V[1]])

    def vbase(pp):
        return C_V if pp % 2 == 0 else C_V2

    def QTb(par):
        return aC[:, C_QT + par * S:C_QT + (par + 1) * S]

    def KTb(par):
        return aC[:, C_KT + par * S:C_KT + (par + 1) * S]

    def load_pair_w(pp):
        p.emit("gpsimd", [lambda e, pp=pp: e.dma_start(
            out=bvr_s[:], in_=bvr_d[0:1, pp * 128:(pp + 1) * 128])],
            writes=[b_bvp], dma=True)
        for wdst, wsrc, bb in [(wq_s, wqp_d, b_wq), (wk_s, wkp_d, b_wk),
                               (wv_s, wvp_d, b_wv)]:
            p.emit("gpsimd", [lambda e, wdst=wdst, wsrc=wsrc, pp=pp: e.dma_start(
                out=wdst[:].rearrange("p (c m) -> p c m", c=DCH),
                in_=wsrc[pp * D:(pp + 1) * D, :].rearrange("(c p) m -> p c m", p=128))],
                writes=[bb], dma=True)

    def qkv_group(pp, kind, qb, bank_ap, bank_buf):
        par = pp % 2
        wsrc = wq_s if kind == 0 else wk_s
        dst = QTb(par) if kind == 0 else KTb(par)
        bdst = b_QT[par] if kind == 0 else b_KT[par]
        bwsrc = b_wq if kind == 0 else b_wk
        bcol = bq8_s[:, pp:pp + 1] if kind == 0 else bkp_s[:, pp:pp + 1]
        scl = 0.125 if kind == 0 else 1.0
        tiles = [b_xnTa[t] for t in range(qb * 4, qb * 4 + 4)]
        p.emit("tensor", [
            (lambda e, j=j, qb=qb, wsrc=wsrc, bank_ap=bank_ap: e.matmul(
                bank_ap, wsrc[:, j * 128:(j + 1) * 128],
                axnT(j)[:, qb * 512:(qb + 1) * 512],
                start=(j == 0), stop=(j == DCH - 1)))
            for j in range(DCH)
        ], reads=[bwsrc] + tiles, writes=[bank_buf])
        p.emit("vector", [lambda e, qb=qb, dst=dst, bank_ap=bank_ap, bcol=bcol,
               scl=scl: e.tensor_scalar(
                   dst[:, qb * 512:(qb + 1) * 512], bank_ap,
                   scl, bcol, op0=ALU.mult, op1=ALU.add)],
               reads=[bank_buf, b_wsm], writes=[bdst])

    def v_group(pp, kt):
        mm = [(lambda e, j=j, kt=kt: e.matmul(
            PC[2][:, 0:128], axnT(j)[:, kt * 128:(kt + 1) * 128],
            wv_s[:, j * 128:(j + 1) * 128],
            start=(j == 0), stop=False)) for j in range(DCH)]
        mm.append(lambda e: e.matmul(
            PC[2][:, 0:128], onesb_s[0:1, :], bvr_s[0:1, :],
            start=False, stop=True))
        p.emit("tensor", mm, reads=[b_wv, b_xnTa[kt], b_consts, b_bvp],
               writes=[b_pc[2]])
        vb = vbase(pp) + kt * 132
        p.emit("vector", [
            lambda e, vb=vb: e.tensor_copy(aC[:, vb:vb + 64], PC[2][:, 0:64]),
            lambda e, vb=vb: e.tensor_copy(aC[:, vb + 66:vb + 130],
                                           PC[2][:, 64:128]),
        ], reads=[b_pc[2]], writes=[b_V[pp % 2]])

    def v_phase(pp):
        for kt in range(NKT):
            v_group(pp, kt)

    def stab_phase(pp):
        par = pp % 2
        QT, KT = QTb(par), KTb(par)
        for kind in range(2):
            src2 = QT if kind == 0 else KT
            bsrc = b_QT[par] if kind == 0 else b_KT[par]
            for qb in range(NQB):
                spar = 0
                sq = aC[:, C_sq:C_sq + 512]
                p.emit("scalar", [lambda e, src2=src2, sq=sq, qb=qb: e.activation(
                    sq, src2[:, qb * 512:(qb + 1) * 512], AF.Square)],
                    reads=[bsrc], writes=[b_sq[spar]])
                for half in range(2):
                    npar = (qb * 2 + half) % 2
                    nbank = PC[2][0:1, 0:512] if npar == 0 else PA[1][0:1, 0:512]
                    nbuf = b_pc[2] if npar == 0 else b_pa[1][0]
                    p.emit("tensor", [lambda e, half=half, sq=sq, nbank=nbank:
                           e.matmul(
                        nbank, onesb_s[half * 64:half * 64 + 64, 0:1],
                        sq[half * 64:half * 64 + 64, :],
                        start=True, stop=True, tile_position=(half * 64, 0))],
                        reads=[b_sq[spar], b_consts], writes=[nbuf])
                    off = 0 if half == 0 else 2 * NQB + 4
                    p.emit("vector", [lambda e, off=off, kind=kind, qb=qb,
                           nbank=nbank: e.tensor_reduce(
                               sqm_s[0:1, off + kind * NQB + qb:
                                     off + kind * NQB + qb + 1],
                               nbank, AX.X, ALU.max)],
                           reads=[nbuf], writes=[b_sqm])
        for half in range(2):
            off = 0 if half == 0 else 2 * NQB + 4
            for kind in range(2):
                p.emit("vector", [lambda e, off=off, kind=kind, half=half:
                       e.tensor_reduce(
                           sqm_s[0:1, 2 * NQB + half * 2 + kind:
                                 2 * NQB + half * 2 + kind + 1],
                           sqm_s[0:1, off + kind * NQB:off + (kind + 1) * NQB],
                           AX.X, ALU.max)],
                       reads=[b_sqm], writes=[b_sqm])
            p.emit("vector", [lambda e, half=half: e.tensor_tensor(
                dnl_s[0:1, half:half + 1],
                sqm_s[0:1, 2 * NQB + half * 2:2 * NQB + half * 2 + 1],
                sqm_s[0:1, 2 * NQB + half * 2 + 1:2 * NQB + half * 2 + 2],
                ALU.mult)], reads=[b_sqm], writes=[b_dnl])
        p.emit("scalar", [lambda e: e.activation(dnl_s[0:1, 2:4], dnl_s[0:1, 0:2],
                                                 AF.Ln)],
               reads=[b_dnl], writes=[b_dnl])
        p.emit("scalar", [lambda e: e.activation(dnl_s[0:1, 4:6], dnl_s[0:1, 2:4],
                                                 AF.Exp, scale=0.5)],
               writes=[b_dnl])
        p.emit("vector", [lambda e: e.tensor_scalar(
            dnr_s[0:1, 512:514], dnl_s[0:1, 4:6], -1.0, None, op0=ALU.mult)],
            reads=[b_dnl], writes=[b_dnl])
        p.emit("tensor", [lambda e: e.matmul(
            PC[2][0:128, 0:2], onesb_s[0:1, :], dnr_s[0:1, 512:514],
            start=True, stop=True)],
            reads=[b_dnl, b_consts], writes=[b_pc[2]])
        p.emit("scalar", [lambda e: e.copy(negc_s[:], PC[2][:, 0:2])],
               reads=[b_pc[2]], writes=[b_negc])

    load_pair_w(0)
    for kind in range(2):
        for qb in range(NQB):
            qkv_group(0, kind, qb, PC[qb % 2][:, :], b_pc[qb % 2])
    v_phase(0)
    stab_phase(0)

    for pp in range(NPAIR):
        par = pp % 2
        QT, KT = QTb(par), KTb(par)
        if pp + 1 < NPAIR:
            load_pair_w(pp + 1)
        qkv_backlog = []
        if pp + 1 < NPAIR:
            for kind in range(2):
                for qb in range(NQB):
                    qkv_backlog.append(
                        lambda kind=kind, qb=qb: qkv_group(
                            pp + 1, kind, qb, PC[2][:, :], b_pc[2]))
            for kt in range(NKT):
                qkv_backlog.append(lambda kt=kt: v_group(pp + 1, kt))

        if ATTN_STAGE == 2:
            continue
        if ATTN_STAGE == 3:
            continue
        for qb in range(NQB):
            abl = []
            for kt in range(NKT):
                sub = kt % 2
                for half in range(2):
                    p.emit("tensor", [lambda e, half=half, kt=kt, qb=qb, sub=sub,
                           QT=QT, KT=KT: e.matmul(
                               PA[half][:, sub * 512:sub * 512 + 512],
                               KT[half * 64:half * 64 + 64,
                                  kt * 128:(kt + 1) * 128],
                               QT[half * 64:half * 64 + 64,
                                  qb * 512:(qb + 1) * 512],
                               start=True, stop=True,
                               tile_position=(half * 64, 0))],
                           reads=[b_QT[par], b_KT[par]],
                           writes=[b_pa[half][sub]])
                for half in range(2):
                    ptb = C_PT + (half * 2 + sub) * 512
                    p.emit("scalar", [lambda e, half=half, ptb=ptb, sub=sub:
                           e.activation(
                        aC[:, ptb:ptb + 512], PA[half][:, sub * 512:sub * 512 + 512],
                        AF.Exp, bias=negc_s[:, half:half + 1])],
                        reads=[b_pa[half][sub], b_negc],
                        writes=[b_PT[half * 2 + sub]])

                    def av(kt=kt, half=half, ptb=ptb, sub=sub, qb=qb):
                        ab = PC[half]
                        bab = b_pc[half]
                        vb = vbase(pp) + kt * 132 + half * 66
                        p.emit("tensor", [lambda e, ab=ab, vb=vb, ptb=ptb, kt=kt:
                               e.matmul(
                                   ab[0:65, :], aC[:, vb:vb + 65],
                                   aC[:, ptb:ptb + 512],
                                   start=(kt == 0), stop=(kt == NKT - 1))],
                               reads=[b_PT[half * 2 + sub], b_V[pp % 2]],
                               writes=[bab])
                    abl.append(av)
                if kt >= 1:
                    abl.pop(0)()
                    abl.pop(0)()
                if qkv_backlog and kt % 2 == 1:
                    qkv_backlog.pop(0)()
            while abl:
                abl.pop(0)()

            for half in range(2):
                ab = PC[half]
                bab = b_pc[half]
                rpar = (qb * 2 + half) % 2
                rawb = aC[rpar * 64:rpar * 64 + 64, C_raw:C_raw + 512]
                p.emit("scalar", [lambda e, ab=ab: e.activation(
                    dnl_s[0:1, 16:528], ab[64:65, :], AF.Ln)],
                    reads=[bab], writes=[b_dnl])
                p.emit("vector", [lambda e, ab=ab, rawb=rawb: e.tensor_copy(
                    rawb, ab[0:64, :])],
                    reads=[bab], writes=[b_rdnb[rpar]])
                p.emit("scalar", [lambda e: e.activation(
                    dnr_s[0:1, 0:512], dnl_s[0:1, 16:528], AF.Exp, scale=-1.0)],
                    writes=[b_dnl])
                rb = PALL[:, 1024 + half * 512:1024 + half * 512 + 512]
                p.emit("tensor", [lambda e, rb=rb: e.matmul(
                    rb, onesb_s[0:1, :], dnr_s[0:1, 0:512],
                    start=True, stop=True)],
                    reads=[b_dnl, b_consts], writes=[b_pa[1][half]])
                p.emit("vector", [lambda e, rb=rb, rawb=rawb, pp=pp, half=half,
                       qb=qb: e.tensor_tensor(
                           aAT(pp)[half * 64:half * 64 + 64,
                                   qb * 512:(qb + 1) * 512],
                           rawb, rb[0:64, :], ALU.mult)],
                       reads=[b_pa[1][half], b_rdnb[rpar]], writes=[b_AT])
        while qkv_backlog:
            qkv_backlog.pop(0)()
        if pp + 1 < NPAIR:
            stab_phase(pp + 1)

    for j in range(DCH):
        p.emit("gpsimd", [lambda e, j=j: e.dma_start(
            out=awoutT(j), in_=woutT_d[j * 128:(j + 1) * 128, :])],
            writes=[b_woutT[j]] + b_xnTa, dma=True)
    for i in range(NT):
        par = i % 2
        xt = xt_s[:, par * D:(par + 1) * D]
        xo = xo_s[:, 0:D]
        if par == 0:
            banks = [(PC[0][:, :], b_pc[0]), (PC[1][:, :], b_pc[1])]
        else:
            banks = [(PA[0][:, 0:512], b_pa[0][0]), (PA[0][:, 512:1024], b_pa[0][1])]
        p.emit("sync", [lambda e, i=i, xt=xt: e.dma_start(
            out=xt, in_=x1_d[i * 128:(i + 1) * 128, :])],
            writes=[b_xt[par]], dma=True)
        for dm in range(2):
            pbk, bbk = banks[dm]
            mm = [(lambda e, ppx=ppx, i=i, dm=dm, pbk=pbk: e.matmul(
                pbk, aAT(ppx)[:, i * 128:(i + 1) * 128],
                awoutT(ppx)[:, dm * 512:(dm + 1) * 512],
                start=(ppx == 0), stop=False)) for ppx in range(NPAIR)]
            mm.append(lambda e, dm=dm, pbk=pbk: e.matmul(
                pbk, onesb_s[0:1, :], boutr_s[:, dm * 512:(dm + 1) * 512],
                start=False, stop=True))
            p.emit("tensor", mm,
                   reads=[b_AT, b_consts, b_wsm] + b_woutT, writes=[bbk])
        p.emit("vector", [
            lambda e, xo=xo, xt=xt, banks=banks: e.tensor_tensor(
                xo[:, 0:512], banks[0][0], xt[:, 0:512], ALU.add),
            lambda e, xo=xo, xt=xt, banks=banks: e.tensor_tensor(
                xo[:, 512:1024], banks[1][0], xt[:, 512:1024], ALU.add),
        ], reads=[banks[0][1], banks[1][1], b_xt[par]], writes=[b_xo[0]])
        p.emit("sync", [lambda e, i=i, xo=xo: e.dma_start(
            out=x2_d[i * 128:(i + 1) * 128, :], in_=xo)],
            reads=[b_xo[0]], writes=[b_x2t[i]], dma=True)

    if n_phases == 2:
        finish(nc, p, ctx)
        return nc

    def toks_of(bufs):
        out = []
        for b in bufs:
            if b.ready is not None:
                out.append(b.ready)
            out.extend(b.readers)
        return out

    attn_toks = toks_of(
        b_QT + b_KT + b_V + [b_sq[0], b_AT, b_negc, b_sqm, b_dnl] + b_PT + b_rdnb)
    at_toks = toks_of([b_AT] + b_woutT)
    xnta_toks = toks_of(b_xnTa)
    for b in (b_xn + b_xnT + b_hsum + b_hsumT + b_basebf + b_tmp + b_xna + [b_wsm]):
        b.readers.extend(attn_toks)
    for j in range(DCH):
        for k in range(4):
            b_wiT[j][k].readers.extend(xnta_toks)
            if j < 2:
                b_wiT[j][k].readers.extend(at_toks)
    for j in range(FF // 128):
        b_woT[j].readers.extend(at_toks)

    load_ffn_weights()
    ffn_phase(x2_d, b_x2t, y_out, None)

    finish(nc, p, ctx)
    return nc


def finish(nc, p, ctx):
    for q in ("sync", "scalar", "gpsimd"):
        for s in p.dma_pools[q]:
            if s.n > 0:
                p.raw_wait("sync", (s, s.n))
    with nc.Block() as block:
        @block.sync
        def _(e):
            for f in p.ops["sync"]:
                f(e)

        @block.tensor
        def _(e):
            for f in p.ops["tensor"]:
                f(e)

        @block.vector
        def _(e):
            for f in p.ops["vector"]:
                f(e)

        @block.scalar
        def _(e):
            for f in p.ops["scalar"]:
                f(e)

        @block.gpsimd
        def _(e):
            for f in p.ops["gpsimd"]:
                f(e)
    ctx.close()


def kernel(**inputs):
    S = 4096
    x = np.asarray(inputs["L0_attn_x"], np.float32)
    lay = host_layout(inputs)
    nc = build_nc(S)
    in_maps = []
    for c in range(NCORES):
        m = {"x": np.ascontiguousarray(x[c])}
        m.update(lay)
        in_maps.append(m)
    res = run_bass_kernel_spmd(nc, in_maps, core_ids=list(range(NCORES)))
    out = np.stack([res.results[c]["y"] for c in range(NCORES)], axis=0)
    return out.astype(np.float32)
